# version 24
# baseline (speedup 1.0000x reference)
import contextlib
import numpy as np
import ml_dtypes
import concourse.bass as bass
import concourse.mybir as mybir
from concourse.bass_utils import run_bass_kernel_spmd

F32 = mybir.dt.float32
BF16 = mybir.dt.bfloat16
F16 = mybir.dt.float16
AF = mybir.ActivationFunctionType
OP = mybir.AluOpType
AX = mybir.AxisListType

NCORES = 8
D = 1024
HD = 4096
NT = 4096
NTILE = 16
NFRONT = 17
XROWS = 4356
RND32 = 1.5 * 2.0 ** 23
R16 = 1536.0
EPS = 1e-5
PS = 260
ACLIP = 2.5e-3

ENGS = ("sync", "vector", "scalar", "tensor", "gpsimd")


class B:

    def __init__(self, nc):
        self.nc = nc
        self.frozen = False
        self.ops = {e: [] for e in ENGS}
        self.cnt = {}
        self.sem_names = []
        for s in ("dx", "dw", "dw2", "dsm", "dst", "dmi", "ddb",
                  "sv", "ss", "sp", "sg", "scc"):
            self.cnt[s] = 0
            self.sem_names.append(s)

    def freeze(self):
        self.frozen = True

    def op(self, eng, fn, inc=None, n=1):
        if self.frozen:
            return self.cnt[inc] if inc else None
        self.ops[eng].append(("op", fn, inc, n))
        if inc is not None:
            self.cnt[inc] += n
        return self.cnt[inc] if inc else None

    def dma(self, eng, out, in_, inc):
        return self.op(eng, lambda e, o=out, i=in_: e.dma_start(out=o, in_=i),
                       inc, 16)

    def wait(self, eng, sem, val=None):
        if self.frozen:
            return
        v = self.cnt[sem] if val is None else val
        self.ops[eng].append(("wait", sem, v, None))

    def drain(self, eng):
        if self.frozen:
            return
        self.ops[eng].append(("drain", None, None, None))

    def bar(self, eng, sem, inc=1):
        if self.frozen:
            self.cnt[sem] += 0
            return self.cnt[sem]
        lst = self.ops[eng]
        if lst and lst[-1][0] == "op" and lst[-1][2] is None:
            kind, fn, _, _ = lst[-1]
            self.cnt[sem] += inc
            lst[-1] = ("op", fn, sem, inc)
            return self.cnt[sem]
        self.drain(eng)
        return self.op(eng, lambda e: e.nop(), sem, inc)

    def mark(self, sem):
        return self.cnt[sem]

    def replay(self, eng_name, eng, sems):
        for kind, a, b_, c in self.ops[eng_name]:
            if kind == "op":
                ins = a(eng)
                if b_ is not None:
                    ins.then_inc(sems[b_], c)
            elif kind == "wait":
                eng.wait_ge(sems[a], b_)
            else:
                eng.drain()


def build_nc(debug=False, phase=99, ntiles=NTILE):
    nc = bass.Bass("TRN2", target_bir_lowering=False)
    p = B(nc)

    def di(name, shape, dt=F32):
        return nc.dram_tensor(name, shape, dt, kind="ExternalInput").ap()

    dx = di("x", [XROWS, D])
    dw1 = di("w1", [HD, D])
    dw2 = di("w2", [D, HD])
    dwin = di("win", [D, 80])
    dbin = di("bin", [1, 80])
    dcw1 = di("cw1", [128, 32])
    dcb1 = di("cb1", [128, 8])
    dwoa = di("woa", [17, D])
    dcw2 = di("cw2", [128, 96])
    dcb2 = di("cb2", [128, 32])
    dalog = di("alog", [16, 1])
    didf = di("idf", [128, 128])
    donef = di("onef", [1, 128])
    donec = di("onec", [128, 1])
    dmconv = di("mconv", [128, 1])
    dmleft = di("mleft", [128, 1])
    dmright = di("mright", [128, 1])
    dcmask = di("cmask", [16, 8])
    dconstv = di("constv", [128, 9])
    donesr = di("onesr", [1, 2049])

    dout = nc.dram_tensor("y_out", [NT, D], F32, kind="ExternalOutput").ap()
    dbg = {}
    if debug:
        for nm, shp, dt in [("d_A", [16, 4098], F32),
                        ("d_C", [16, 2050], F32), ("d_h", [16, 4098], F32),
                        ("d_y0", [128, D // 2], F32), ("d_x2", [128, D // 2], F32),
                        ("d_P0", [128, 16 * PS], F32), ("d_Hq", [128, 8 * 256], F32),
                        ("d_scl", [1, 8], F32), ("d_xln", [128, D // 2], F32),
                        ("d_wink", [128, 1280], F32), ("d_At", [16, 512], F32),
                        ("d_wp", [128, 80], F32),
                        ("d_edge", [2, D // 2], F32)]:
            dbg[nm] = nc.dram_tensor(nm, shp, dt, kind="ExternalOutput").ap()

    cc_in = nc.dram_tensor("cc_in", [16, 2], F32)
    cc_out = nc.dram_tensor("cc_out", [128, 2], F32)

    ctx = contextlib.ExitStack()
    E = ctx.enter_context

    W1qT = E(nc.sbuf_tensor("W1qT", [128, 8 * HD], BF16))
    W2qT = E(nc.sbuf_tensor("W2qT", [128, 32 * D], F16))
    Ch = E(nc.sbuf_tensor("Ch", [17, 4098], BF16))
    WoaB = E(nc.sbuf_tensor("WoaB", [17, D], BF16))
    identb = E(nc.sbuf_tensor("s_identb", [128, 128], BF16))
    identh = E(nc.sbuf_tensor("s_identh", [128, 128], F16))
    onef = E(nc.sbuf_tensor("s_onef", [1, 128], F32))
    onec = E(nc.sbuf_tensor("s_onec", [128, 1], F32))
    oneh = E(nc.sbuf_tensor("s_oneh", [1, 128], F16))
    cw1 = E(nc.sbuf_tensor("s_cw1", [128, 32], F32))
    cb1 = E(nc.sbuf_tensor("s_cb1", [128, 8], F32))
    cw2 = E(nc.sbuf_tensor("s_cw2", [128, 96], F32))
    cb2 = E(nc.sbuf_tensor("s_cb2", [128, 32], F32))
    mconv = E(nc.sbuf_tensor("s_mconv", [128, 1], F32))
    mleft = E(nc.sbuf_tensor("s_mleft", [128, 1], F32))
    mright = E(nc.sbuf_tensor("s_mright", [128, 1], F32))
    cmask = E(nc.sbuf_tensor("s_cmask", [16, 8], F32))
    mhalf = E(nc.sbuf_tensor("mhalf", [128, 2], F32))
    scl = E(nc.sbuf_tensor("scl", [128, 8], F32))
    sclr = E(nc.sbuf_tensor("sclr", [1, 8], F32))
    constv = E(nc.sbuf_tensor("s_constv", [128, 9], F32))
    Pedge = E(nc.sbuf_tensor("Pedge", [128, 64], F16))
    cw2h = E(nc.sbuf_tensor("s_cw2h", [128, 96], F16))

    p_ya = E(nc.psum_tensor("p_ya", [128, 512], F32))
    p_tr = E(nc.psum_tensor("p_tr", [128, 512], F32))
    p_m1 = [E(nc.psum_tensor(f"p_m1{k}", [128, 512], F32)) for k in range(2)]
    p_m2 = [E(nc.psum_tensor(f"p_m2{k}", [128, 512], F32)) for k in range(4)]

    for o, i in [(onef[:, :], donef),
                 (onec[:, :], donec),
                 (cw1[:, :], dcw1), (cb1[:, :], dcb1),
                 (cw2[:, :], dcw2), (cb2[:, :], dcb2),
                 (mconv[:, :], dmconv), (mleft[:, :], dmleft),
                 (mright[:, :], dmright), (cmask[:, :], dcmask),
                 (constv[:, :], dconstv)]:
        p.dma("sync", o, i, "dsm")
    p.wait("vector", "dsm", p.mark("dsm"))
    p.op("vector", lambda v: v.tensor_copy(oneh[:, :], onef[:, :]))
    p.op("vector", lambda v: v.tensor_copy(cw2h[:, :], cw2[:, :]))
    p.bar("vector", "sv")
    consts_cast_sv = p.mark("sv")
    for k, val in enumerate([EPS, 1.0, RND32, -0.5, -1.0, 0.0, R16, -R16,
                             -RND32]):
        nc.const_aps.register(F32, val, constv[:, k:k + 1])
    consts_dsm = p.mark("dsm")
    p.op("gpsimd", lambda g: g.memset(mhalf[:, :], -0.5))
    p.bar("gpsimd", "sg")
    mhalf_sg = p.mark("sg")

    wctx = contextlib.ExitStack()
    EW = wctx.enter_context
    wsts = [EW(nc.sbuf_tensor(f"wst{b}", [128, 4096], F32)) for b in range(2)]
    terns = [EW(nc.sbuf_tensor(f"tern{b}", [128, 4096], BF16))
             for b in range(2)]
    wst = wsts[0]
    wsum = EW(nc.sbuf_tensor("wsum", [128, 16], F32))
    wcol = EW(nc.sbuf_tensor("wcol", [128, 2], F32))
    wsem = ["dw", "dw2"]

    p.wait("sync", "dsm", 0)
    p.dma("sync", wsts[0][:, 0:128], didf, "dw")
    p.wait("vector", "dw", p.mark("dw"))
    p.op("vector", lambda v: v.tensor_copy(identb[:, :], wsts[0][:, 0:128]))
    p.op("vector", lambda v: v.tensor_copy(identh[:, :], wsts[0][:, 0:128]))
    ident_sv = p.bar("vector", "sv")

    def w_src(wi, g, b):
        if wi == 0:
            return (dw1[512 * g:512 * (g + 1), :]
                    .rearrange("(s q) d -> q s d", s=4),
                    wsts[b][:, :].rearrange("q (s d) -> q s d", s=4))
        return dw2[128 * g:128 * (g + 1), :], wsts[b][:, :]

    wgs = [(wi, g) for wi in range(2) for g in range(8)]
    red_done = []
    dmamk = []

    def p1_red(idx):
        wi, g = wgs[idx]
        sem, mk = dmamk[idx]
        p.wait("vector", sem, mk)
        p.op("vector", lambda v, c=8 * wi + g, b=idx % 2: v.tensor_reduce(
            wsum[:, c:c + 1], wsts[b][:, :], AX.X, OP.add,
            apply_absolute_value=True))
        red_done.append(p.bar("vector", "sv"))

    for idx, (wi, g) in enumerate(wgs):
        b = idx % 2
        src, dst = w_src(wi, g, b)
        if idx == 0:
            p.wait("sync", "sv", ident_sv)
        if idx >= 2:
            p.wait("sync", "sv", red_done[idx - 2])
        p.dma("sync", dst, src, wsem[b])
        dmamk.append((wsem[b], p.mark(wsem[b])))
        if idx >= 1:
            p1_red(idx - 1)
    p1_red(15)
    p.op("vector", lambda v: v.tensor_reduce(
        wcol[:, 0:1], wsum[:, 0:8], AX.X, OP.add))
    p.op("vector", lambda v: v.tensor_reduce(
        wcol[:, 1:2], wsum[:, 8:16], AX.X, OP.add))
    p.bar("vector", "sv")
    p.wait("tensor", "sv", p.mark("sv"))
    p.wait("tensor", "dsm", consts_dsm)
    for k in range(2):
        p.op("tensor", lambda t, k=k: t.matmul(
            p_ya[0:1, k:k + 1], wcol[:, k:k + 1], onec[:, 0:1],
            start=True, stop=True))
    p.bar("tensor", "sp")
    p.wait("vector", "sp", p.mark("sp"))
    p.op("vector", lambda v: v.tensor_scalar(
        sclr[:, 0:1], p_ya[0:1, 0:1], 1.0 / (HD * D), 1e-5, OP.mult, OP.max))
    p.op("vector", lambda v: v.tensor_scalar(
        sclr[:, 1:2], p_ya[0:1, 1:2], 1.0 / (HD * D), 1e-5, OP.mult, OP.max))
    p.drain("vector")
    p.op("vector", lambda v: v.reciprocal(sclr[:, 2:3], sclr[:, 0:1]))
    p.op("vector", lambda v: v.reciprocal(sclr[:, 3:4], sclr[:, 1:2]))
    p.drain("vector")
    p.op("vector", lambda v: v.tensor_scalar(
        sclr[:, 4:5], sclr[:, 0:1], 1.0 / 127.0, None, OP.mult))
    p.op("vector", lambda v: v.tensor_scalar(
        sclr[:, 5:6], sclr[:, 1:2], 1.0 / 127.0, None, OP.mult))
    p.bar("vector", "sv")
    p.wait("tensor", "sv", p.mark("sv"))
    for k, c in [(2, 0), (3, 1), (4, 2), (5, 3)]:
        p.op("tensor", lambda t, k=k, c=c: t.matmul(
            p_tr[:, c:c + 1], onef[:, :], sclr[:, k:k + 1],
            start=True, stop=True))
    p.bar("tensor", "sp")
    p.wait("vector", "sp", p.mark("sp"))
    for c in range(4):
        p.op("vector", lambda v, c=c: v.tensor_copy(
            scl[:, c:c + 1], p_tr[:, c:c + 1]))
    mean_sv = p.bar("vector", "sv")
    if debug:
        p.wait("sync", "sv", p.mark("sv"))
        p.dma("sync", dbg["d_scl"], sclr[:, :], "ddb")

    ev_marks = []
    tern_built = []
    p1mk = dmamk

    def p2_body(idx):
        wi, g = wgs[idx]
        b = idx % 2
        sem, mk = dmamk[order.index(idx)]
        p.wait("scalar", sem, mk)
        p.wait("scalar", "sv", mean_sv)
        p.op("scalar", lambda s, c=wi, b=b: s.activation(
            wsts[b][:, :], wsts[b][:, :], AF.Copy, bias=RND32,
            scale=scl[:, c:c + 1]))
        p.op("scalar", lambda s, b=b: s.activation(
            wsts[b][:, :], wsts[b][:, :], AF.Copy, bias=-RND32))
        p.bar("scalar", "ss")
        p.wait("vector", "ss", p.mark("ss"))
        p.op("vector", lambda v, b=b: v.tensor_scalar(
            terns[b][:, :], wsts[b][:, :], 1.0, -1.0, OP.min, OP.max))
        tern_built.append(p.bar("vector", "sv"))
        p.wait("tensor", "sv", tern_built[-1])
        for bt in range(8):
            pb = p_tr if bt % 2 == 0 else p_m1[0]
            if len(ev_marks) >= 2:
                p.wait("tensor", "sv", ev_marks[-2])
            for q in range(4):
                ti = 4 * bt + q
                if wi == 0:
                    s4, dc = ti // 8, ti % 8
                    lhsT = terns[b][:, 1024 * s4 + 128 * dc:
                                    1024 * s4 + 128 * (dc + 1)]
                else:
                    lhsT = terns[b][:, 128 * ti:128 * (ti + 1)]
                idt = identb[:, :]
                p.op("tensor", lambda t, pb=pb, q=q, l=lhsT, idt=idt:
                     t.matmul(pb[:, 128 * q:128 * (q + 1)], l, idt,
                              start=True, stop=True))
            mmk = p.bar("tensor", "sp")
            p.wait("vector", "sp", mmk)
            for q in range(4):
                ti = 4 * bt + q
                if wi == 0:
                    s4, dc = ti // 8, ti % 8
                    s = 4 * g + s4
                    dd = W1qT[:, HD * dc + 128 * s:HD * dc + 128 * (s + 1)]
                else:
                    dd = W2qT[:, D * ti + 128 * g:D * ti + 128 * (g + 1)]
                p.op("vector", lambda v, pb=pb, q=q, dd=dd:
                     v.tensor_copy(dd, pb[:, 128 * q:128 * (q + 1)]))
            ev_marks.append(p.bar("vector", "sv"))

    order = [14, 15] + list(range(14))
    dmamk = []
    for pos, idx in enumerate(order):
        wi, g = wgs[idx]
        b = pos % 2
        assert idx % 2 == b
        if pos < 2:
            dmamk.append(p1mk[idx])
        else:
            p.wait("sync", "sv", tern_built[pos - 2])
            src, dst = w_src(wi, g, b)
            p.dma("sync", dst, src, wsem[b])
            dmamk.append((wsem[b], p.mark(wsem[b])))
        if pos >= 1:
            p2_body(order[pos - 1])
    p2_body(order[15])
    wq_sv = p.mark("sv")
    if phase < 2:
        p.freeze()

    wctx.close()
    actx = contextlib.ExitStack()
    EA = actx.enter_context
    hbuf = EA(nc.sbuf_tensor("hbuf", [16, 4098], F32))
    acum = EA(nc.sbuf_tensor("acum", [16, 4098], F32))
    Cp = EA(nc.sbuf_tensor("Cp", [16, 4100], BF16))
    xlnfm = EA(nc.sbuf_tensor("xlnfm", [128, 8 * PS], BF16))
    xa = EA(nc.sbuf_tensor("xa", [128, D], F32))
    xa2 = EA(nc.sbuf_tensor("xa2", [128, D], F32))
    xlna = EA(nc.sbuf_tensor("xlna", [128, 2 * D], BF16))
    sp16 = EA(nc.sbuf_tensor("sp16", [16, 5 * 256], F32))
    wpad = xa[:, 128:208]
    wink = EA(nc.sbuf_tensor("wink", [128, 32 * 80], BF16))
    biasc = Pedge[:, :].bitcast(F32)[0:80, 0:4]
    a16 = EA(nc.sbuf_tensor("a16", [16, 34], F32))
    biasc0 = a16[:, 31:34]
    rowb = sp16[0:1, 0:336]
    acol = a16[:, 0:12]
    dAacc = a16[:, 12:29]

    ccin_sb = a16[:, 29:31]
    stat = EA(nc.sbuf_tensor("stat", [128, 28], F32))
    gath = stat[0:16, 0:16]

    p.dma("sync", acol[:, 3:4], dalog, "dsm")
    p.wait("scalar", "dsm", p.mark("dsm"))
    p.op("scalar", lambda s: s.activation(acol[:, 3:4], acol[:, 3:4], AF.Exp))
    p.bar("scalar", "ss")
    p.wait("vector", "ss", p.mark("ss"))
    p.op("vector", lambda v: v.tensor_scalar(
        acol[:, 1:2], acol[:, 3:4], 1e-4, None, OP.is_lt))
    p.drain("vector")
    p.op("vector", lambda v: v.tensor_scalar(
        acol[:, 0:1], acol[:, 3:4], -1.0, None, OP.mult))
    p.drain("vector")
    p.op("vector", lambda v: v.tensor_scalar(
        acol[:, 5:6], acol[:, 0:1], -1.0, 1.0, OP.mult, OP.add))
    p.drain("vector")
    p.op("vector", lambda v: v.scalar_tensor_tensor(
        acol[:, 5:6], acol[:, 5:6], acol[:, 1:2], acol[:, 0:1],
        OP.mult, OP.add))
    p.drain("vector")
    p.op("vector", lambda v: v.reciprocal(acol[:, 2:3], acol[:, 5:6]))
    p.drain("vector")

    def avg_pairs(dst, src):
        p.op("vector", lambda v, d=dst, s=src: v.tensor_tensor(
            d, s.rearrange("q (a b) -> q a b", b=2)[:, :, 0:1],
            s.rearrange("q (a b) -> q a b", b=2)[:, :, 1:2], OP.add))

    p.wait("vector", "dsm", consts_dsm)
    p.op("vector", lambda v: v.memset(wpad[:, :], 0.0))
    p.drain("vector")
    for dc in range(8):
        p.wait("sync", "sv", p.mark("sv"))
        p.dma("sync", xa[:, 0:80], dwin[128 * dc:128 * (dc + 1), :], "dsm")
        p.wait("vector", "dsm", p.mark("dsm"))
        p.op("vector", lambda v: v.tensor_copy(wpad[:, 0:16], xa[:, 0:16]))
        avg_pairs(wpad[:, 32:48], xa[:, 16:48])
        avg_pairs(wpad[:, 64:80], xa[:, 48:80])
        p.drain("vector")
        p.op("vector", lambda v: v.tensor_scalar(
            wpad[:, 32:48], wpad[:, 32:48], 0.5, None, OP.mult))
        p.op("vector", lambda v: v.tensor_scalar(
            wpad[:, 64:80], wpad[:, 64:80], 0.5, None, OP.mult))
        p.drain("vector")
        for k in range(4):
            p.op("vector", lambda v, dc=dc, k=k: v.tensor_scalar(
                wink[:, 80 * (4 * dc + k):80 * (4 * dc + k + 1)], wpad[:, 0:80],
                cw1[:, 4 * dc + k:4 * dc + k + 1], None, OP.mult))
        wk_sv = p.bar("vector", "sv")
        p.wait("tensor", "sv", wk_sv)
        p.op("tensor", lambda t, dc=dc: t.matmul(
            p_ya[0:1, 8:88], cb1[:, dc:dc + 1], wpad[:, 0:80],
            start=(dc == 0), stop=(dc == 7)))
    p.bar("tensor", "sp")
    p.wait("sync", "sv", p.mark("sv"))
    p.dma("sync", rowb[0:1, 256:336], dbin, "dsm")
    p.wait("vector", "dsm", p.mark("dsm"))
    p.op("vector", lambda v: v.memset(rowb[0:1, 0:256], 0.0))
    p.drain("vector")
    p.op("vector", lambda v: v.tensor_copy(rowb[0:1, 0:16], rowb[0:1, 256:272]))
    avg_pairs(rowb[0:1, 32:48], rowb[0:1, 272:304])
    avg_pairs(rowb[0:1, 64:80], rowb[0:1, 304:336])
    p.drain("vector")
    p.op("vector", lambda v: v.tensor_scalar(
        rowb[0:1, 32:48], rowb[0:1, 32:48], 0.5, None, OP.mult))
    p.op("vector", lambda v: v.tensor_scalar(
        rowb[0:1, 64:80], rowb[0:1, 64:80], 0.5, None, OP.mult))
    p.drain("vector")
    p.wait("vector", "sp", p.mark("sp"))
    p.op("vector", lambda v: v.tensor_tensor(
        rowb[0:1, 80:160], p_ya[0:1, 8:88], rowb[0:1, 0:80], OP.add))
    p.bar("vector", "sv")
    p.wait("tensor", "sv", p.mark("sv"))
    p.op("tensor", lambda t: t.matmul(
        p_tr[0:80, 0:1], rowb[0:1, 80:160], onef[0:1, 0:1],
        start=True, stop=True))
    p.op("tensor", lambda t: t.matmul(
        p_tr[0:80, 1:2], rowb[0:1, 0:80], onef[0:1, 0:1],
        start=True, stop=True))
    for k, off in ((0, 80), (1, 112), (2, 144)):
        p.op("tensor", lambda t, k=k, off=off: t.matmul(
            p_tr[0:16, 4 + k:5 + k], rowb[0:1, off:off + 16], onef[0:1, 0:1],
            start=True, stop=True))
    p.bar("tensor", "sp")
    p.wait("vector", "sp", p.mark("sp"))
    p.op("vector", lambda v: v.tensor_copy(biasc[:, 0:1], p_tr[0:80, 0:1]))
    p.op("vector", lambda v: v.tensor_copy(biasc[:, 1:2], p_tr[0:80, 1:2]))
    for k in range(3):
        p.op("vector", lambda v, k=k: v.tensor_copy(
            biasc0[:, k:k + 1], p_tr[0:16, 4 + k:5 + k]))
    p.drain("vector")
    p.op("vector", lambda v: v.tensor_tensor(
        biasc[:, 2:3], biasc[:, 1:2], biasc[:, 0:1], OP.subtract))
    p.drain("vector")
    p.op("vector", lambda v: v.tensor_scalar(
        biasc[:, 3:4], biasc[:, 2:3], -1.0, None, OP.mult))
    p.drain("vector")
    p.wait("sync", "sv", p.mark("sv"))
    p.dma("sync", xa[0:17, 0:D], dwoa, "dsm")
    p.wait("vector", "dsm", p.mark("dsm"))
    p.op("vector", lambda v: v.tensor_copy(WoaB[:, :], xa[0:17, 0:D]))
    consts_sv = p.bar("vector", "sv")
    if debug:
        p.wait("sync", "sv", consts_sv)
        p.dma("sync", dbg["d_wink"], wink[:, :].bitcast(F32), "ddb")
        p.dma("sync", dbg["d_wp"], wpad[:, :], "ddb")

    def ln1(xsl, dst):
        p.op("vector", lambda v: v.bn_stats(stat[:, 0:6], xsl[:, 0:512]))
        p.op("vector", lambda v: v.bn_stats(stat[:, 6:12], xsl[:, 512:1024]))
        p.drain("vector")
        p.op("vector", lambda v: v.bn_aggr(stat[:, 12:14], stat[:, 0:12]))
        p.drain("vector")
        sv = p.bar("vector", "sv")
        p.wait("scalar", "sv", sv)
        p.op("scalar", lambda s: s.activation(
            stat[:, 14:15], stat[:, 13:14], AF.Ln, bias=EPS))
        p.drain("scalar")
        p.op("scalar", lambda s: s.activation(
            stat[:, 15:16], stat[:, 14:15], AF.Exp, scale=-0.5))
        ssm = p.bar("scalar", "ss")
        p.wait("vector", "ss", ssm)
        p.op("vector", lambda v: v.tensor_scalar(
            stat[:, 16:17], stat[:, 12:13], stat[:, 15:16], -1.0,
            OP.mult, OP.mult))
        p.op("vector", lambda v: v.tensor_scalar(
            dst, xsl, stat[:, 15:16], stat[:, 16:17], OP.mult, OP.add))
        return p.bar("vector", "sv")

    def transpose_batch(srcs, dsts, ident, prev_ev):
        if prev_ev is not None:
            p.wait("tensor", "sv", prev_ev)
        for q, s in enumerate(srcs):
            p.op("tensor", lambda t, q=q, s=s: t.matmul(
                p_tr[:, 128 * q:128 * (q + 1)], s, ident,
                start=True, stop=True))
        mk = p.bar("tensor", "sp")
        p.wait("vector", "sp", mk)
        for q, dd in enumerate(dsts):
            p.op("vector", lambda v, q=q, dd=dd: v.tensor_copy(
                dd, p_tr[:, 128 * q:128 * (q + 1)]))
        return p.bar("vector", "sv")

    p.wait("sync", "sv", p.mark("sv"))
    p.dma("sync", xa[:, :], dx[0:128, :], "dx")
    p.wait("vector", "dx", p.mark("dx"))
    p.wait("vector", "sv", consts_sv)
    pre_ss = ln1(xa[:, :], xlna[:, 0:D])
    p.wait("tensor", "sv", pre_ss)
    ev = None
    for half in range(2):
        srcs = [xlna[:, 128 * (4 * half + q):128 * (4 * half + q + 1)]
                for q in range(4)]
        dsts = [xlnfm[:, PS * (4 * half + q) + 2:PS * (4 * half + q) + 4]
                for q in range(4)]
        if ev is not None:
            p.wait("tensor", "sv", ev)
        for q, s in enumerate(srcs):
            p.op("tensor", lambda t, q=q, s=s: t.matmul(
                p_tr[:, 128 * q:128 * (q + 1)], s, identb[:, :],
                start=True, stop=True))
        mk = p.bar("tensor", "sp")
        p.wait("vector", "sp", mk)
        for q, dd in enumerate(dsts):
            p.op("vector", lambda v, q=q, dd=dd: v.tensor_copy(
                dd, p_tr[:, 128 * q:128 * q + 2]))
        ev = p.bar("vector", "sv")
    p.op("vector", lambda v: v.memset(
        xlnfm[:, :].rearrange("q (c w) -> q c w", w=PS)[:, :, 0:1], 0.0))
    p.bar("vector", "sv")

    ep_done = {}
    mm_mk = {}
    xl_mk = {}

    def ln1_pair(c):
        p.wait("sync", "sv", p.mark("sv"))
        row0 = 256 * c + 2
        p.dma("sync", xa[:, :], dx[row0:row0 + 128, :], "dx")
        p.dma("sync", xa2[:, :], dx[row0 + 128:row0 + 256, :], "dx")
        p.wait("vector", "dx", p.mark("dx"))
        p.wait("vector", "sv", consts_sv)
        p.op("vector", lambda v: v.bn_stats(stat[:, 0:6], xa[:, 0:512]))
        p.op("vector", lambda v: v.bn_stats(stat[:, 6:12], xa[:, 512:1024]))
        p.op("vector", lambda v: v.bn_stats(stat[:, 12:18], xa2[:, 0:512]))
        p.op("vector", lambda v: v.bn_stats(stat[:, 18:24], xa2[:, 512:1024]))
        p.drain("vector")
        p.op("vector", lambda v: v.bn_aggr(stat[:, 24:26], stat[:, 0:12]))
        p.op("vector", lambda v: v.bn_aggr(stat[:, 26:28], stat[:, 12:24]))
        sv = p.bar("vector", "sv")
        p.wait("scalar", "sv", sv)
        p.op("scalar", lambda s: s.activation(
            stat[:, 0:4], stat[:, 24:28], AF.Ln, bias=EPS))
        p.drain("scalar")
        p.op("scalar", lambda s: s.activation(
            stat[:, 4:8], stat[:, 0:4], AF.Exp, scale=-0.5))
        ssm = p.bar("scalar", "ss")
        p.wait("vector", "ss", ssm)
        for sb in range(2):
            p.op("vector", lambda v, sb=sb: v.tensor_scalar(
                stat[:, 8 + sb:9 + sb], stat[:, 24 + 2 * sb:25 + 2 * sb],
                stat[:, 5 + 2 * sb:6 + 2 * sb], -1.0, OP.mult, OP.mult))
        p.drain("vector")
        for sb, xsl in enumerate([xa, xa2]):
            p.op("vector", lambda v, xsl=xsl, sb=sb: v.tensor_scalar(
                xlna[:, D * sb:D * (sb + 1)], xsl[:, :],
                stat[:, 5 + 2 * sb:6 + 2 * sb],
                stat[:, 8 + sb:9 + sb], OP.mult, OP.add))
        mk = p.bar("vector", "sv")
        if debug and c == 0:
            p.wait("sync", "sv", mk)
            p.dma("sync", dbg["d_xln"], xlna[:, 0:D].bitcast(F32), "ddb")
        return mk

    def stage1(c):
        ss_ln = ln1_pair(c)
        if c > 0:
            p.wait("vector", "sp", mm_mk[c - 1])
            p.op("vector", lambda v: v.tensor_copy(
                xlnfm[:, :].rearrange("q (cc w) -> q cc w", w=PS)[:, :, 1:4],
                xlnfm[:, :].rearrange("q (cc w) -> q cc w", w=PS)[:, :,
                                                                  257:260]))
            p.bar("vector", "sv")
        p.wait("tensor", "sv", ss_ln)
        ev = None
        for s_ in range(2):
            for half in range(2):
                srcs = [xlna[:, D * s_ + 128 * (4 * half + q):
                             D * s_ + 128 * (4 * half + q + 1)]
                        for q in range(4)]
                dsts = [xlnfm[:, PS * (4 * half + q) + 4 + 128 * s_:
                              PS * (4 * half + q) + 4 + 128 * (s_ + 1)]
                        for q in range(4)]
                ev = transpose_batch(srcs, dsts, identb[:, :], ev)
        pp = p_m1[c % 2]
        p.wait("tensor", "sv", ev)
        if c >= 2:
            p.wait("tensor", "sv", ep_done[c - 2])
        for dc in range(8):
            for k in range(4):
                p.op("tensor", lambda t, dc=dc, k=k, pp=pp: t.matmul(
                    pp[0:80, 0:256],
                    wink[:, 80 * (4 * dc + k):80 * (4 * dc + k + 1)],
                    xlnfm[:, PS * dc + 1 + k:PS * dc + 257 + k],
                    start=(dc == 0 and k == 0), stop=(dc == 7 and k == 3)))
        mm_mk[c] = p.bar("tensor", "sp")

    def stage2(c):
        pp = p_m1[c % 2]
        mm = mm_mk[c]
        col0 = 256 * c
        W = 256 if c < NFRONT - 1 else 4
        sl = [sp16[:, 256 * k:256 * (k + 1)] for k in range(5)]
        e = [sl[0], sl[1], sl[2], sl[1], sl[0], sl[1], sl[2]]
        pa = sl[3][:, 0:W]
        pb_ = sl[4][:, 0:W]
        pc_ = Cp[0:16, col0:col0 + W]
        if c == NFRONT - 1:
            p.wait("vector", "sp", mm)
            p.op("vector", lambda v, pp=pp: v.tensor_scalar(
                pp[0:80, 1:2], pp[0:80, 1:2], biasc[:, 3:4],
                mconv[0:80, 0:1], OP.add, OP.mult))
            p.drain("vector")
            p.op("vector", lambda v, pp=pp: v.tensor_tensor(
                pp[0:80, 1:2], pp[0:80, 1:2], biasc[0:80, 2:3], OP.add))
            mm = p.bar("vector", "sv")
            p.wait("scalar", "sv", mm)
        else:
            p.wait("scalar", "sp", mm)
        p.op("scalar", lambda s, pp=pp: s.activation(
            e[0], pp[0:16, 0:256], AF.Abs, bias=biasc0[:, 0:1]))
        p.op("scalar", lambda s, pp=pp: s.activation(
            e[2], pp[0:16, 0:256], AF.Relu, bias=biasc0[:, 0:1]))
        p.drain("scalar")
        p.op("scalar", lambda s: s.activation(e[1], e[0], AF.Exp, scale=-1.0))
        p.drain("scalar")
        p.op("scalar", lambda s: s.activation(e[1], e[1], AF.Ln, bias=1.0))
        sm = p.bar("scalar", "ss")
        p.wait("vector", "ss", sm)
        p.op("vector", lambda v: v.tensor_tensor(e[3], e[1], e[2], OP.add))
        p.drain("vector")
        p.op("vector", lambda v: v.tensor_scalar(
            e[4], e[3], acol[:, 0:1], None, OP.mult))
        p.drain("vector")
        rsl = e[4][:, 2:256] if c == 0 else (
            e[4][:, 0:2] if c == NFRONT - 1 else e[4])
        p.op("vector", lambda v, r=rsl, c=c: v.tensor_reduce(
            dAacc[:, c:c + 1], r, AX.X, OP.add))
        sv = p.bar("vector", "sv")
        p.wait("scalar", "sv", sv)
        p.op("scalar", lambda s, pa=pa, W=W: s.activation(
            pa, e[4][:, 0:W], AF.Exp))
        sm = p.bar("scalar", "ss")
        p.wait("vector", "ss", sm)
        p.op("vector", lambda v: v.tensor_scalar(
            e[5], e[4], 1.0 / 6.0, 0.5, OP.mult, OP.add))
        p.drain("vector")
        p.op("vector", lambda v: v.tensor_tensor(e[5], e[5], e[4], OP.mult))
        p.op("vector", lambda v, pa=pa, W=W: v.tensor_scalar(
            e[6][:, 0:W], pa, -1.0, acol[:, 2:3], OP.add, OP.mult))
        p.drain("vector")
        p.op("vector", lambda v: v.tensor_scalar(e[5], e[5], 1.0, None,
                                                 OP.add))
        p.drain("vector")
        p.op("vector", lambda v: v.tensor_tensor(e[5], e[5], e[6],
                                                 OP.subtract))
        p.drain("vector")
        p.op("vector", lambda v: v.scalar_tensor_tensor(
            e[5], e[5], acol[:, 1:2], e[6], OP.mult, OP.add))
        p.drain("vector")
        p.op("vector", lambda v, pp=pp, d_=pb_, W=W: v.scalar_tensor_tensor(
            d_, pp[32:48, 0:W], biasc0[:, 1:2], e[5][:, 0:W],
            OP.add, OP.mult))
        p.op("vector", lambda v, pp=pp, d_=pc_, W=W: v.tensor_scalar(
            d_, pp[64:80, 0:W], biasc0[:, 2:3], None, OP.add))
        ep_done[c] = p.bar("vector", "sv")
        if debug and c == 1:
            p.wait("sync", "sv", ep_done[c])
            p.dma("sync", dbg["d_At"], sp16[:, 3 * 256:5 * 256], "ddb")
            p.wait("scalar", "ddb", p.mark("ddb"))
            p.wait("vector", "ddb", p.mark("ddb"))
        if c > 0:
            p.op("vector", lambda v, c=c, sl=sl: v.tensor_tensor_scan(
                hbuf[:, 256 * c - 1:256 * c + 1], sl[3][:, 0:2],
                sl[4][:, 0:2], hbuf[:, 256 * c - 2:256 * c - 1],
                OP.mult, OP.add))
            p.op("vector", lambda v, c=c, sl=sl: v.tensor_tensor_scan(
                acum[:, 256 * c - 1:256 * c + 1], sl[3][:, 0:2],
                sl[3][:, 0:2], acum[:, 256 * c - 2:256 * c - 1],
                OP.mult, OP.bypass))
            p.drain("vector")
        if c < NFRONT - 1:
            if c == 0:
                p.op("vector", lambda v, sl=sl: v.tensor_tensor_scan(
                    hbuf[:, 1:255], sl[3][:, 2:256], sl[4][:, 2:256], 0.0,
                    OP.mult, OP.add))
                p.op("vector", lambda v, sl=sl: v.tensor_tensor_scan(
                    acum[:, 1:255], sl[3][:, 2:256], sl[3][:, 2:256], 1.0,
                    OP.mult, OP.bypass))
            else:
                p.op("vector", lambda v, c=c, sl=sl: v.tensor_tensor_scan(
                    hbuf[:, 256 * c + 1:256 * c + 255], sl[3][:, 2:256],
                    sl[4][:, 2:256], hbuf[:, 256 * c:256 * c + 1],
                    OP.mult, OP.add))
                p.op("vector", lambda v, c=c, sl=sl: v.tensor_tensor_scan(
                    acum[:, 256 * c + 1:256 * c + 255], sl[3][:, 2:256],
                    sl[3][:, 2:256], acum[:, 256 * c:256 * c + 1],
                    OP.mult, OP.bypass))
            p.drain("vector")
        else:
            p.op("vector", lambda v, sl=sl: v.tensor_copy(
                acol[:, 9:10], sl[3][:, 2:3]))
            p.op("vector", lambda v, sl=sl: v.tensor_copy(
                acol[:, 10:11], sl[4][:, 2:3]))
            p.drain("vector")
        ep_done[c] = p.bar("vector", "sv")

    for c in range(NFRONT):
        stage1(c)
        if c > 0:
            stage2(c - 1)
    stage2(NFRONT - 1)

    if phase < 3:
        p.freeze()
    p.op("vector", lambda v: v.tensor_reduce(
        acol[:, 6:7], dAacc[:, 0:NFRONT], AX.X, OP.add))
    sv = p.bar("vector", "sv")
    p.wait("scalar", "sv", sv)
    p.op("scalar", lambda s: s.activation(acol[:, 7:8], acol[:, 6:7], AF.Exp))
    sm = p.bar("scalar", "ss")
    p.wait("vector", "ss", sm)
    p.op("vector", lambda v: v.tensor_copy(ccin_sb[:, 0:1], acol[:, 7:8]))
    p.op("vector", lambda v: v.tensor_copy(ccin_sb[:, 1:2],
                                           hbuf[:, 4096:4097]))
    sv = p.bar("vector", "sv")
    p.wait("gpsimd", "sv", sv)
    p.dma("gpsimd", cc_in.ap(), ccin_sb[:, :], "dmi")
    p.wait("gpsimd", "dmi", p.mark("dmi"))
    p.op("gpsimd", lambda g: g.collective_compute(
        "AllGather", OP.bypass, replica_groups=[list(range(NCORES))],
        ins=[cc_in.ap()], outs=[cc_out.ap()]), "scc")
    p.wait("gpsimd", "scc", p.mark("scc"))
    p.dma("gpsimd", gath[:, :].rearrange("n (r k) -> n r k", k=2),
          cc_out.ap().rearrange("(r n) k -> n r k", r=8, n=16, k=2), "dmi")
    p.wait("vector", "dmi", p.mark("dmi"))
    p.op("vector", lambda v: v.memset(acol[:, 4:5], 0.0))
    p.drain("vector")
    for r in range(7):
        p.op("vector", lambda v, r=r: v.scalar_tensor_tensor(
            acol[:, 5:6], acol[:, 4:5], gath[:, 2 * r:2 * r + 1],
            gath[:, 2 * r + 1:2 * r + 2], OP.mult, OP.add))
        p.drain("vector")
        p.op("vector", lambda v: v.tensor_tensor(
            acol[:, 5:6], acol[:, 5:6], acol[:, 4:5], OP.subtract))
        p.drain("vector")
        p.op("vector", lambda v, r=r: v.scalar_tensor_tensor(
            acol[:, 4:5], acol[:, 5:6], cmask[:, r:r + 1], acol[:, 4:5],
            OP.mult, OP.add))
        p.drain("vector")
    p.op("vector", lambda v: v.scalar_tensor_tensor(
        hbuf[:, 1:4097], acum[:, 1:4097], acol[:, 4:5], hbuf[:, 1:4097],
        OP.mult, OP.add))
    p.drain("vector")
    p.op("vector", lambda v: v.tensor_copy(hbuf[:, 0:1], acol[:, 4:5]))
    p.drain("vector")
    p.op("vector", lambda v: v.scalar_tensor_tensor(
        hbuf[:, 4097:4098], hbuf[:, 4096:4097], acol[:, 9:10],
        acol[:, 10:11], OP.mult, OP.add))
    p.drain("vector")
    p.wait("sync", "sv", p.mark("sv"))
    p.dma("sync", Ch[16:17, :].bitcast(F32), donesr, "dsm")
    p.op("vector", lambda v: v.tensor_tensor(
        Ch[0:16, 0:1], Cp[0:16, 1:2], hbuf[:, 0:1], OP.mult))
    p.op("vector", lambda v: v.tensor_tensor(
        Ch[0:16, 1:2], Cp[0:16, 4098:4099], hbuf[:, 4097:4098], OP.mult))
    p.op("vector", lambda v: v.tensor_tensor(
        Ch[0:16, 2:4098], Cp[0:16, 2:4098], hbuf[:, 1:4097], OP.mult))
    p.drain("vector")
    p.op("vector", lambda v: v.nop(), "sv")
    p.wait("vector", "dsm", p.mark("dsm"))
    scan_sv = p.bar("vector", "sv")
    if debug:
        p.wait("sync", "sv", scan_sv)
        p.dma("sync", dbg["d_A"], acum[:, 0:4098], "ddb")
        p.dma("sync", dbg["d_C"], Cp[0:16, :].bitcast(F32), "ddb")
        p.dma("sync", dbg["d_h"], hbuf[:, :], "ddb")

    actx.close()

    cctx = contextlib.ExitStack()
    EC = cctx.enter_context
    P0 = EC(nc.sbuf_tensor("P0", [128, 32 * PS], F16))
    P1 = EC(nc.sbuf_tensor("P1", [128, 32 * PS], F16))
    H2 = EC(nc.sbuf_tensor("H2", [128, 16 * 256], F16))
    outt = H2[:, 2048:3072].bitcast(F32)
    outt2 = H2[:, 3072:4096].bitcast(F32)
    ysbB = H2[:, 0:2048].bitcast(BF16)

    def x2keep(t):
        return Pb[t % 2][:, 0:2048].bitcast(BF16)
    xcf = EC(nc.sbuf_tensor("xcf", [128, 2 * D], F32))
    x2t = [EC(nc.sbuf_tensor(f"x2_{k}", [128, 2 * D], BF16)) for k in range(2)]
    ysb = EC(nc.sbuf_tensor("ysb", [128, 2 * D], BF16))
    xqfm = EC(nc.sbuf_tensor("xqfm", [128, 8 * 256], BF16))
    qf = xqfm[:, :].bitcast(F16)
    s2b = EC(nc.sbuf_tensor("s2b", [128, 256], F16))
    cst = EC(nc.sbuf_tensor("cst", [128, 44], F32))
    s2h = EC(nc.sbuf_tensor("s2h", [128, 2], F16))
    srh = EC(nc.sbuf_tensor("srh", [1, 256], F16))
    xe = xcf[0:2, D:2 * D]
    ee = xcf[0:2, 0:D]
    Pb = [P0, P1]


    def ln_pow(parts):
        src, P = parts
        p.op("vector", lambda v: v.bn_stats(cst[0:P, 0:6], src[:, 0:512]))
        p.op("vector", lambda v: v.bn_stats(cst[0:P, 6:12], src[:, 512:1024]))
        p.drain("vector")
        p.op("vector", lambda v: v.bn_aggr(cst[0:P, 12:14], cst[0:P, 0:12]))
        p.drain("vector")
        p.op("vector", lambda v: v.tensor_scalar(
            cst[0:P, 14:15], cst[0:P, 13:14], EPS, None, OP.add))
        sv = p.bar("vector", "sv")
        p.wait("gpsimd", "sv", sv)
        p.op("gpsimd", lambda g: g.tensor_tensor(
            cst[0:P, 15:16], cst[0:P, 14:15], mhalf[0:P, 0:1], OP.pow))
        sg = p.bar("gpsimd", "sg")
        p.wait("vector", "sg", sg)
        p.op("vector", lambda v: v.tensor_scalar(
            cst[0:P, 16:17], cst[0:P, 12:13], cst[0:P, 15:16], -1.0,
            OP.mult, OP.mult))
        p.drain("vector")

    def ln_pair(src0, src1):
        p.op("vector", lambda v: v.bn_stats(cst[:, 0:6], src0[:, 0:512]))
        p.op("vector", lambda v: v.bn_stats(cst[:, 6:12], src0[:, 512:1024]))
        p.op("vector", lambda v: v.bn_stats(cst[:, 12:18], src1[:, 0:512]))
        p.op("vector", lambda v: v.bn_stats(cst[:, 18:24], src1[:, 512:1024]))
        p.drain("vector")
        p.op("vector", lambda v: v.bn_aggr(cst[:, 24:26], cst[:, 0:12]))
        p.op("vector", lambda v: v.bn_aggr(cst[:, 26:28], cst[:, 12:24]))
        p.drain("vector")
        mv = cst[:, 24:28].rearrange("q (a b) -> q a b", b=2)
        p.op("vector", lambda v, mv=mv: v.tensor_scalar(
            cst[:, 28:30].rearrange("q (a b) -> q a b", b=1),
            mv[:, :, 1:2], EPS, None, OP.add))
        sv = p.bar("vector", "sv")
        p.wait("gpsimd", "sv", sv)
        p.op("gpsimd", lambda g: g.tensor_tensor(
            cst[:, 30:32], cst[:, 28:30], mhalf[:, 0:2], OP.pow))
        sg = p.bar("gpsimd", "sg")
        p.wait("vector", "sg", sg)
        p.op("vector", lambda v, mv=mv: v.tensor_tensor(
            cst[:, 32:34].rearrange("q (a b) -> q a b", b=1),
            mv[:, :, 0:1],
            cst[:, 30:32].rearrange("q (a b) -> q a b", b=1), OP.mult))
        p.drain("vector")

    def front_chain2(i):
        x2b = x2t[i % 2]
        ln_pair(ysb[:, 0:D], ysb[:, D:2 * D])
        if i - 2 >= 0 and i - 2 in m2k_done:
            p.wait("vector", "sp", m2k_done[i - 2])
            p.op("vector", lambda v, j=i - 2: v.tensor_copy(
                x2keep(j)[:, :], x2t[j % 2][:, :]))
            p.drain("vector")
        for sb in range(2):
            p.op("vector", lambda v, sb=sb, x2b=x2b: v.scalar_tensor_tensor(
                x2b[:, D * sb:D * (sb + 1)], ysb[:, D * sb:D * (sb + 1)],
                cst[:, 30 + sb:31 + sb], xcf[:, D * sb:D * (sb + 1)],
                OP.mult, OP.add))
        p.drain("vector")
        for sb in range(2):
            p.op("vector", lambda v, sb=sb, x2b=x2b: v.tensor_scalar(
                x2b[:, D * sb:D * (sb + 1)], x2b[:, D * sb:D * (sb + 1)],
                cst[:, 32 + sb:33 + sb], None, OP.subtract))
        p.drain("vector")
        ln_pair(x2b[:, 0:D], x2b[:, D:2 * D])
        for sb in range(2):
            p.op("vector", lambda v, sb=sb, x2b=x2b: v.tensor_scalar(
                ysb[:, D * sb:D * (sb + 1)], x2b[:, D * sb:D * (sb + 1)],
                cst[:, 30 + sb:31 + sb], cst[:, 32 + sb:33 + sb],
                OP.mult, OP.subtract))
        p.drain("vector")
        for sb in range(2):
            p.op("vector", lambda v, sb=sb: v.tensor_reduce(
                cst[:, 34 + sb:35 + sb], ysb[:, D * sb:D * (sb + 1)],
                AX.X, OP.max, apply_absolute_value=True))
        p.drain("vector")
        p.op("vector", lambda v: v.tensor_scalar(
            cst[:, 34:36], cst[:, 34:36], 1e-5, None, OP.max))
        p.drain("vector")
        p.op("vector", lambda v: v.reciprocal(cst[:, 36:38], cst[:, 34:36]))
        p.drain("vector")
        p.op("vector", lambda v: v.tensor_scalar(
            cst[:, 36:38], cst[:, 36:38], 127.0, None, OP.mult))
        p.op("vector", lambda v: v.tensor_scalar(
            cst[:, 38:40], cst[:, 34:36], scl[:, 2:3], None, OP.mult))
        p.drain("vector")
        for sb in range(2):
            p.op("vector", lambda v, sb=sb: v.tensor_scalar(
                qf[:, D * sb:D * (sb + 1)], ysb[:, D * sb:D * (sb + 1)],
                cst[:, 36 + sb:37 + sb], R16, OP.mult, OP.add))
        p.drain("vector")
        p.op("vector", lambda v: v.tensor_scalar(
            qf[:, 0:2 * D], qf[:, 0:2 * D], -R16, None, OP.add))
        p.drain("vector")
        for sb in range(2):
            p.op("vector", lambda v, sb=sb: v.tensor_scalar(
                ysb[:, D * sb:D * (sb + 1)], qf[:, D * sb:D * (sb + 1)],
                cst[:, 38 + sb:39 + sb], None, OP.mult))
        return p.bar("vector", "sv")

    def front_chain(P, ybuf, xbuf, x2buf, xqbuf, qbuf):
        w = D
        ln_pow((ybuf, P))
        p.op("vector", lambda v: v.scalar_tensor_tensor(
            x2buf, ybuf, cst[0:P, 15:16], xbuf, OP.mult, OP.add))
        p.drain("vector")
        p.op("vector", lambda v: v.tensor_scalar(
            x2buf, x2buf, cst[0:P, 16:17], None, OP.add))
        p.drain("vector")
        ln_pow((x2buf, P))
        p.op("vector", lambda v: v.tensor_scalar(
            ybuf, x2buf, cst[0:P, 15:16], cst[0:P, 16:17], OP.mult, OP.add))
        p.drain("vector")
        p.op("vector", lambda v: v.tensor_reduce(
            cst[0:P, 17:18], ybuf, AX.X, OP.max, apply_absolute_value=True))
        p.drain("vector")
        p.op("vector", lambda v: v.tensor_scalar(
            cst[0:P, 17:18], cst[0:P, 17:18], 1e-5, None, OP.max))
        p.drain("vector")
        p.op("vector", lambda v: v.reciprocal(cst[0:P, 18:19],
                                              cst[0:P, 17:18]))
        p.drain("vector")
        p.op("vector", lambda v: v.tensor_scalar(
            cst[0:P, 18:19], cst[0:P, 18:19], 127.0, None, OP.mult))
        p.op("vector", lambda v: v.tensor_scalar(
            cst[0:P, 19:20], cst[0:P, 17:18], scl[0:P, 2:3], None, OP.mult))
        p.drain("vector")
        p.op("vector", lambda v: v.tensor_scalar(
            qbuf, ybuf, cst[0:P, 18:19], R16, OP.mult, OP.add))
        p.drain("vector")
        p.op("vector", lambda v: v.tensor_scalar(
            qbuf, qbuf, -R16, None, OP.add))
        p.drain("vector")
        p.op("vector", lambda v: v.tensor_scalar(
            xqbuf, qbuf, cst[0:P, 19:20], None, OP.mult))
        return p.bar("vector", "sv")

    if phase < 4:
        p.freeze()
    p.wait("tensor", "sv", scan_sv)
    p.wait("tensor", "sv", wq_sv)
    for hf in range(2):
        pbk = [p_ya, p_tr][hf]
        p.op("tensor", lambda t, hf=hf, pbk=pbk:
             t.matmul(pbk[0:2, 0:512], Ch[:, 0:2],
                      WoaB[:, 512 * hf:512 * (hf + 1)],
                      start=True, stop=True))
    mk = p.bar("tensor", "sp")
    p.wait("scalar", "sp", mk)
    for hf in range(2):
        pbk = [p_ya, p_tr][hf]
        p.op("scalar", lambda s, hf=hf, pbk=pbk: s.copy(
            ee[0:2, 512 * hf:512 * (hf + 1)], pbk[0:2, 0:512]))
    p.bar("scalar", "ss")
    p.wait("sync", "sv", p.mark("sv"))
    p.dma("sync", xe[0:1, :], dx[1:2, :], "dx")
    p.dma("sync", xe[1:2, :], dx[4098:4099, :], "dx")
    p.wait("vector", "ss", p.mark("ss"))
    p.wait("vector", "dx", p.mark("dx"))
    p.op("vector", lambda v: v.tensor_copy(ysb[0:2, 0:D], ee[0:2, 0:D]))
    p.bar("vector", "sv")
    esv = front_chain(2, ysb[0:2, 0:D], xe[0:2, :], x2t[0][0:2, 0:D],
                      ysb[0:2, 0:D], qf[0:2, 0:D])
    if debug:
        p.wait("sync", "sv", esv)
        p.dma("sync", dbg["d_edge"], x2t[0][0:2, 0:D].bitcast(F32), "ddb")
    p.wait("tensor", "sv", esv)
    for dc in range(8):
        p.op("tensor", lambda t, dc=dc: t.matmul(
            p_tr[:, 2 * dc:2 * (dc + 1)],
            ysb[0:2, 128 * dc:128 * (dc + 1)], identb[0:2, 0:2],
            start=True, stop=True))
    mk = p.bar("tensor", "sp")
    p.wait("vector", "sp", mk)
    p.op("vector", lambda v: v.tensor_copy(xqfm[:, 0:16], p_tr[:, 0:16]))
    esv = p.bar("vector", "sv")
    p.wait("tensor", "sv", esv)
    for hc in range(32):
        for dc in range(8):
            p.op("tensor", lambda t, hc=hc, dc=dc: t.matmul(
                p_m1[0][:, 2 * hc:2 * (hc + 1)],
                W1qT[:, HD * dc + 128 * hc:HD * dc + 128 * (hc + 1)],
                xqfm[:, 2 * dc:2 * (dc + 1)],
                start=(dc == 0), stop=(dc == 7)))
    mk = p.bar("tensor", "sp")
    p.wait("scalar", "sp", mk)
    p.op("scalar", lambda s: s.copy(Pedge[:, 0:64], p_m1[0][:, 0:64]))
    p.bar("scalar", "ss")
    p.wait("tensor", "ss", p.mark("ss"))
    p.wait("vector", "ss", p.mark("ss"))
    pe3 = Pedge[:, :].rearrange("q (c k) -> q c k", k=2)
    p.op("vector", lambda v: v.tensor_scalar(
        pe3[:, :, 0:1], pe3[:, :, 0:1], mleft[:, 0:1], None, OP.mult))
    p.op("vector", lambda v: v.tensor_scalar(
        pe3[:, :, 1:2], pe3[:, :, 1:2], mright[:, 0:1], None, OP.mult))
    edge_sv = p.bar("vector", "sv")

    m1ev = []
    m2k_done = {}
    fp_ev = {}
    tap_tv = {}
    yev = {}
    xmk = {}
    tail_sv = {}

    def fp_y(i):
        if i - 1 in tap_tv:
            p.wait("sync", "sv", tap_tv[i - 1])
        else:
            p.wait("sync", "sv", p.mark("sv"))
            p.wait("sync", "ss", p.mark("ss"))
        for s in range(2):
            row0 = 256 * i + 128 * s + 2
            p.dma("sync", xcf[:, D * s:D * (s + 1)], dx[row0:row0 + 128, :],
                  "dx")
        xmk[i] = p.mark("dx")
        if i - 1 in fp_ev:
            p.wait("scalar", "sv", fp_ev[i - 1])
            p.wait("tensor", "sv", fp_ev[i - 1])
        else:
            p.wait("scalar", "sv", p.mark("sv"))
            p.wait("tensor", "sv", p.mark("sv"))
        p.wait("tensor", "ss", p.mark("ss"))
        for s in range(2):
            if s == 1:
                p.wait("tensor", "ss", p.mark("ss"))
            for hf in range(2):
                pbk = [p_ya, p_tr][hf]
                p.op("tensor", lambda t, i=i, s=s, hf=hf, pbk=pbk: t.matmul(
                    pbk[:, 0:512], Ch[:, 256 * i + 128 * s + 2:
                                      256 * i + 128 * (s + 1) + 2],
                    WoaB[:, 512 * hf:512 * (hf + 1)], start=True, stop=True))
            mk = p.bar("tensor", "sp")
            p.wait("scalar", "sp", mk)
            for hf in range(2):
                pbk = [p_ya, p_tr][hf]
                p.op("scalar", lambda s_, s=s, hf=hf, pbk=pbk: s_.copy(
                    ysb[:, D * s + 512 * hf:D * s + 512 * (hf + 1)],
                    pbk[:, 0:512]))
            yev[i] = p.bar("scalar", "ss")
        if debug and i == 0:
            p.wait("sync", "ss", p.mark("ss"))
            p.dma("sync", dbg["d_y0"], ysb[:, 0:D].bitcast(F32), "ddb")

    def fp_chain(i):
        p.wait("vector", "ss", yev[i])
        p.wait("vector", "dx", xmk[i])
        fc = front_chain2(i)
        if debug and i == 0:
            p.wait("sync", "sv", fc)
            p.dma("sync", dbg["d_x2"], x2t[0][:, 0:D].bitcast(F32), "ddb")
        p.wait("tensor", "sv", fc)
        ev = None
        for s in range(2):
            for half in range(2):
                srcs = [ysb[:, D * s + 128 * (4 * half + q):
                            D * s + 128 * (4 * half + q + 1)]
                        for q in range(4)]
                dsts = [xqfm[:, 256 * (4 * half + q) + 128 * s:
                             256 * (4 * half + q) + 128 * (s + 1)]
                        for q in range(4)]
                ev = transpose_batch(srcs, dsts, identb[:, :], ev)
        return ev

    r3 = lambda t: t[:, :].rearrange("q (c w) -> q c w", w=PS)
    pe2 = Pedge[:, :].rearrange("q (c k) -> q c k", k=2)

    def m1_group(i, g, ev):
        Pi = Pb[i % 2]
        if g == 0:
            p.wait("tensor", "sv", ev)
            p.wait("scalar", "sv",
                   tail_sv[i - 2] if i - 2 in tail_sv else p.mark("sv"))
        for c16 in range(8):
            hc = 8 * g + c16
            pm = p_m1[hc % 2]
            if len(m1ev) >= 2:
                p.wait("tensor", "ss", m1ev[-2])
            for dc in range(8):
                p.op("tensor", lambda t, hc=hc, dc=dc, pm=pm:
                     t.matmul(pm[:, 0:256],
                              W1qT[:, HD * dc + 128 * hc:
                                   HD * dc + 128 * (hc + 1)],
                              xqfm[:, 256 * dc:256 * (dc + 1)],
                              start=(dc == 0), stop=(dc == 7)))
            mk = p.bar("tensor", "sp")
            p.wait("scalar", "sp", mk)
            p.op("scalar", lambda s, hc=hc, pm=pm, Pi=Pi: s.copy(
                Pi[:, PS * hc + 1:PS * hc + 257], pm[:, 0:256]))
            m1ev.append(p.bar("scalar", "ss"))
        p.wait("vector", "ss", p.mark("ss"))
        g8 = slice(8 * g, 8 * g + 8)
        if i == 0:
            p.op("vector", lambda v, g8=g8: v.tensor_copy(
                r3(Pb[0])[:, g8, 0:1], pe2[:, g8, 0:1]))
        else:
            p.op("vector", lambda v, i=i, g8=g8: v.tensor_copy(
                r3(Pb[i % 2])[:, g8, 0:1],
                r3(Pb[(i - 1) % 2])[:, g8, 256:257]))
            p.op("vector", lambda v, i=i, g8=g8: v.tensor_copy(
                r3(Pb[(i - 1) % 2])[:, g8, 257:258],
                r3(Pb[i % 2])[:, g8, 1:2]))
        return p.bar("vector", "sv")

    def tap_views(t, g):
        Pt = Pb[t % 2]
        rp_ = r3(Pt)
        base = 2048 * (g % 2)
        h3 = H2[:, base:base + 2048].rearrange("q (c w) -> q c w", w=256)
        xch = xcf[:, :].bitcast(F16)
        tmpA = xch[:, 0:2048].rearrange("q (c w) -> q c w", w=256)
        tmpB = xch[:, 2048:4096].rearrange("q (c w) -> q c w", w=256)
        w3 = cw2h[:, :].rearrange("q (c k) -> q c k", k=3)
        g8 = slice(8 * g, 8 * g + 8)
        return Pt, rp_, base, h3, tmpA, tmpB, w3, g8

    def patch_ops(t, g, level):
        Pt, rp_, base, h3, tmpA, tmpB, w3, g8 = tap_views(t, g)
        if level == 0:
            p.op("vector", lambda v, h3=h3, rp_=rp_, w3=w3, g8=g8:
                 v.tensor_tensor(h3[:, :, 255:256], rp_[:, g8, 256:257],
                                 w3[:, g8, 1:2], OP.mult))
            p.op("vector", lambda v, tmpA=tmpA, rp_=rp_, w3=w3, g8=g8:
                 v.tensor_tensor(tmpA[:, :, 255:256], rp_[:, g8, 255:256],
                                 w3[:, g8, 0:1], OP.mult))
        elif level == 1:
            p.op("vector", lambda v, h3=h3, tmpA=tmpA: v.tensor_tensor(
                h3[:, :, 255:256], h3[:, :, 255:256], tmpA[:, :, 255:256],
                OP.add))
            p.op("vector", lambda v, tmpB=tmpB, rp_=rp_, w3=w3, g8=g8:
                 v.tensor_tensor(tmpB[:, :, 255:256], rp_[:, g8, 257:258],
                                 w3[:, g8, 2:3], OP.mult))
        else:
            p.op("vector", lambda v, h3=h3, tmpB=tmpB: v.tensor_tensor(
                h3[:, :, 255:256], h3[:, :, 255:256], tmpB[:, :, 255:256],
                OP.add))

    def taps_main(t, g, sil_marks, halo_prev):
        Pt, rp_, base, h3, tmpA, tmpB, w3, g8 = tap_views(t, g)
        if g == 0:
            p.wait("vector", "dst", p.mark("dst"))
            if t - 1 in m2k_done:
                p.wait("vector", "sp", m2k_done[t - 1])
        if g >= 2 and sil_marks:
            p.wait("vector", "ss", sil_marks[g - 2])
        a1, b1 = bass.broadcast_tensor_aps(rp_[:, g8, 1:256], w3[:, g8, 1:2])
        p.op("vector", lambda v, a=a1, b=b1, h3=h3: v.tensor_tensor(
            h3[:, :, 0:255], a, b, OP.mult))
        a0, b0 = bass.broadcast_tensor_aps(rp_[:, g8, 0:255], w3[:, g8, 0:1])
        p.op("vector", lambda v, a=a0, b=b0, tmpA=tmpA: v.tensor_tensor(
            tmpA[:, :, 0:255], a, b, OP.mult))
        if halo_prev is not None:
            p.wait("vector", "sv", halo_prev)
            patch_ops(t, g - 1, 0)
        p.drain("vector")
        p.op("vector", lambda v, h3=h3, tmpA=tmpA: v.tensor_tensor(
            h3[:, :, 0:255], h3[:, :, 0:255], tmpA[:, :, 0:255], OP.add))
        a2, b2 = bass.broadcast_tensor_aps(rp_[:, g8, 2:257], w3[:, g8, 2:3])
        p.op("vector", lambda v, a=a2, b=b2, tmpB=tmpB: v.tensor_tensor(
            tmpB[:, :, 0:255], a, b, OP.mult))
        if halo_prev is not None:
            patch_ops(t, g - 1, 1)
        p.drain("vector")
        p.op("vector", lambda v, h3=h3, tmpB=tmpB: v.tensor_tensor(
            h3[:, :, 0:255], h3[:, :, 0:255], tmpB[:, :, 0:255], OP.add))
        if halo_prev is not None:
            patch_ops(t, g - 1, 2)
        return p.bar("vector", "sv")

    def silu_only(t, g, mark):
        Pt, rp_, base, h3, tmpA, tmpB, w3, g8 = tap_views(t, g)
        p.wait("scalar", "sv", mark)
        for c16 in range(8):
            c = 8 * g + c16
            p.op("scalar", lambda s_, c=c, c16=c16, base=base: s_.activation(
                Pt[:, PS * c + 2:PS * c + 258],
                H2[:, base + 256 * c16:base + 256 * (c16 + 1)],
                AF.Silu, bias=cb2[:, c:c + 1]))
        return p.bar("scalar", "ss")

    def taps_patch_silu(t, g, halo_sv):
        p.wait("vector", "sv", halo_sv)
        patch_ops(t, g, 0)
        p.drain("vector")
        patch_ops(t, g, 1)
        p.drain("vector")
        patch_ops(t, g, 2)
        tv = p.bar("vector", "sv")
        tap_tv[t] = tv
        return silu_only(t, g, tv)

    def back_head(t, sil):
        Pt = Pb[t % 2]
        rp = r3(Pt)
        if debug and t == 0:
            p.wait("sync", "ss", sil)
            p.dma("sync", dbg["d_P0"], Pt[:, :].bitcast(F32), "ddb")
        p.wait("vector", "ss", sil)
        p.wait("vector", "dst", p.mark("dst"))
        if t - 1 in m2k_done:
            p.wait("vector", "sp", m2k_done[t - 1])
        A3 = H2[:, 0:2048].rearrange("q (c w) -> q c w", w=256)
        B3 = H2[:, 2048:4096].rearrange("q (c w) -> q c w", w=256)
        p.op("vector", lambda v, A3=A3: v.tensor_tensor(
            A3, rp[:, 0:8, 2:258], rp[:, 8:16, 2:258], OP.max))
        p.op("vector", lambda v, B3=B3: v.tensor_tensor(
            B3, rp[:, 0:8, 2:258], rp[:, 8:16, 2:258], OP.min))
        p.drain("vector")
        for lo in (16, 24):
            p.op("vector", lambda v, lo=lo, A3=A3: v.tensor_tensor(
                A3, A3, rp[:, lo:lo + 8, 2:258], OP.max))
            p.op("vector", lambda v, lo=lo, B3=B3: v.tensor_tensor(
                B3, B3, rp[:, lo:lo + 8, 2:258], OP.min))
            p.drain("vector")
        for wd in (1024, 512, 256):
            p.op("vector", lambda v, wd=wd: v.tensor_tensor(
                H2[:, 0:wd], H2[:, 0:wd], H2[:, wd:2 * wd], OP.max))
            p.op("vector", lambda v, wd=wd: v.tensor_tensor(
                H2[:, 2048:2048 + wd], H2[:, 2048:2048 + wd],
                H2[:, 2048 + wd:2048 + 2 * wd], OP.min))
            p.drain("vector")
        p.op("vector", lambda v: v.scalar_tensor_tensor(
            H2[:, 0:256], H2[:, 2048:2304], -1.0, H2[:, 0:256],
            OP.mult, OP.max))
        mx = p.bar("vector", "sv")
        p.wait("tensor", "sv", mx)
        if t + 2 in yev:
            p.wait("tensor", "ss", yev[t + 2])
        for sb in range(2):
            p.op("tensor", lambda t_, sb=sb: t_.matmul(
                p_tr[:, 128 * sb:128 * (sb + 1)],
                H2[:, 128 * sb:128 * (sb + 1)],
                identh[:, :], start=True, stop=True))
        mk = p.bar("tensor", "sp")
        p.wait("vector", "sp", mk)
        for sb in range(2):
            p.op("vector", lambda v, sb=sb: v.tensor_reduce(
                cst[:, 20 + sb:21 + sb],
                p_tr[:, 128 * sb:128 * (sb + 1)], AX.X, OP.max))
        p.drain("vector")
        p.drain("vector")
        p.op("vector", lambda v: v.reciprocal(cst[:, 22:24], cst[:, 20:22]))
        p.drain("vector")
        p.op("vector", lambda v: v.tensor_scalar(
            s2h[:, 0:2], cst[:, 22:24], 127.0, 127.0 / ACLIP,
            OP.mult, OP.min))
        p.op("vector", lambda v: v.tensor_scalar(
            cst[:, 40:42], cst[:, 20:22], scl[:, 3:4], scl[:, 5:6],
            OP.mult, OP.max))
        sv = p.bar("vector", "sv")
        p.wait("tensor", "sv", sv)
        for sb in range(2):
            p.op("tensor", lambda t_, sb=sb: t_.matmul(
                p_tr[0:1, 128 * sb:128 * sb + 128],
                s2h[:, sb:sb + 1], identh[:, :],
                start=True, stop=True))
        mk = p.bar("tensor", "sp")
        p.wait("vector", "sp", mk)
        p.op("vector", lambda v: v.tensor_copy(srh[0:1, 0:256],
                                               p_tr[0:1, 0:256]))
        sv = p.bar("vector", "sv")
        p.wait("tensor", "sv", sv)
        for sb in range(2):
            p.op("tensor", lambda t_, sb=sb: t_.matmul(
                p_ya[:, 128 * sb:128 * (sb + 1)], oneh[:, :],
                srh[0:1, 128 * sb:128 * (sb + 1)], start=True, stop=True))
        mk = p.bar("tensor", "sp")
        p.wait("vector", "sp", mk)
        for sb in range(2):
            p.op("vector", lambda v, sb=sb: v.tensor_copy(
                s2b[:, 128 * sb:128 * (sb + 1)],
                p_ya[:, 128 * sb:128 * (sb + 1)]))
        sv = p.bar("vector", "sv")
        m2mk = {}
        for qq in range(4):
            for j in range(8):
                c = 8 * qq + j
                p.op("vector", lambda v, c=c: v.tensor_tensor(
                    Pt[:, PS * c + 2:PS * c + 258],
                    Pt[:, PS * c + 2:PS * c + 258], s2b[:, :], OP.mult))
            tq = p.bar("vector", "sv")
            p.wait("scalar", "sv", tq)
            sl3 = rp[:, 8 * qq:8 * qq + 8, 2:258]
            p.op("scalar", lambda s_, sl3=sl3: s_.activation(
                sl3, sl3, AF.Copy, bias=R16))
            p.op("scalar", lambda s_, sl3=sl3: s_.activation(
                sl3, sl3, AF.Copy, bias=-R16))
            hq = p.bar("scalar", "ss")
            p.wait("tensor", "ss", hq)
            for j in range(8):
                hc = 8 * qq + j
                for sb in range(2):
                    for hf in range(2):
                        p.op("tensor", lambda t_, hc=hc, sb=sb, hf=hf:
                             t_.matmul(
                                 p_m2[2 * sb + hf][:, 0:512],
                                 Pt[:, PS * hc + 2 + 128 * sb:
                                    PS * hc + 2 + 128 * (sb + 1)],
                                 W2qT[:, D * hc + 512 * hf:
                                      D * hc + 512 * (hf + 1)],
                                 start=(hc == 0), stop=(hc == 31)))
            m2mk[qq] = p.bar("tensor", "sp")
        m2k_done[t] = m2mk[3]

    def back_tail(t):
        p.wait("scalar", "sp", m2k_done[t])
        for sb in range(2):
            for hf in range(2):
                p.op("scalar", lambda s_, sb=sb, hf=hf: s_.activation(
                    ysbB[:, D * sb + 512 * hf:D * sb + 512 * (hf + 1)],
                    p_m2[2 * sb + hf][:, 0:512], AF.Copy,
                    scale=cst[:, 40 + sb:41 + sb]))
        ev = p.bar("scalar", "ss")
        p.wait("vector", "ss", ev)
        ln_pair(ysbB[:, 0:D], ysbB[:, D:2 * D])
        for sb in range(2):
            row0 = 256 * t + 128 * sb
            p.wait("vector", "dst", p.mark("dst"))
            x2src = x2t[t % 2] if t >= ntiles - 2 else x2keep(t)
            for hf, ot in ((0, outt), (1, outt2)):
                p.op("vector", lambda v, sb=sb, hf=hf, ot=ot, x2src=x2src:
                     v.scalar_tensor_tensor(
                         ot[:, :],
                         ysbB[:, D * sb + 512 * hf:D * sb + 512 * (hf + 1)],
                         cst[:, 30 + sb:31 + sb],
                         x2src[:, D * sb + 512 * hf:
                               D * sb + 512 * (hf + 1)],
                         OP.mult, OP.add))
            p.drain("vector")
            for hf, ot in ((0, outt), (1, outt2)):
                p.op("vector", lambda v, sb=sb, ot=ot: v.tensor_scalar(
                    ot[:, :], ot[:, :], cst[:, 32 + sb:33 + sb],
                    None, OP.subtract))
            sv = p.bar("vector", "sv")
            p.wait("sync", "sv", sv)
            p.dma("sync", dout[row0:row0 + 128, 0:512], outt[:, :], "dst")
            p.dma("sync", dout[row0:row0 + 128, 512:1024], outt2[:, :], "dst")
        tail_sv[t] = p.mark("sv")

    if phase < 5:
        p.freeze()
    p.wait("vector", "sv", edge_sv)
    sil_marks = []
    fp_y(0)
    for i in range(ntiles + 2):
        if i < ntiles:
            fp_ev[i] = fp_chain(i)
        if i >= 2:
            back_tail(i - 2)
        if i > ntiles:
            continue
        halos = {}
        for g in range(4):
            if i >= 1:
                tm = taps_main(i - 1, g, sil_marks, halos.get(g - 1))
            if i < ntiles:
                halo = m1_group(i, g, fp_ev[i])
            else:
                p.wait("vector", "ss", p.mark("ss"))
                g8 = slice(8 * g, 8 * g + 8)
                p.op("vector", lambda v, g8=g8: v.tensor_copy(
                    r3(Pb[(ntiles - 1) % 2])[:, g8, 257:258],
                    pe2[:, g8, 1:2]))
                halo = p.bar("vector", "sv")
            halos[g] = halo
            if i >= 1 and g >= 1:
                sil_marks.append(silu_only(i - 1, g - 1, tm))
        if i >= 1:
            sil_marks.append(taps_patch_silu(i - 1, 3, halos[3]))
        if i + 1 < ntiles:
            fp_y(i + 1)
        if i >= 1:
            back_head(i - 1, sil_marks[-1])
        sil_marks = []

    p.wait("sync", "dst", p.mark("dst"))
    p.wait("sync", "ddb", p.mark("ddb"))

    semctx = contextlib.ExitStack()
    sems = {s: semctx.enter_context(nc.semaphore(s)) for s in p.sem_names}
    with nc.Block() as block:
        @block.sync
        def _(e):
            p.replay("sync", e, sems)

        @block.vector
        def _(e):
            p.replay("vector", e, sems)

        @block.scalar
        def _(e):
            p.replay("scalar", e, sems)

        @block.tensor
        def _(e):
            p.replay("tensor", e, sems)

        @block.gpsimd
        def _(e):
            p.replay("gpsimd", e, sems)
    semctx.close()
    cctx.close()
    ctx.close()
    return nc


def _stage_inputs(inputs):
    x = np.asarray(inputs["L0_attn_x"], np.float32)
    w1 = np.ascontiguousarray(np.asarray(inputs["L0_ffn_w1"], np.float32))
    w2 = np.ascontiguousarray(np.asarray(inputs["L0_ffn_w2"], np.float32))
    win = np.ascontiguousarray(np.asarray(inputs["L0_attn_Win"], np.float32))
    binp = np.asarray(inputs["L0_attn_b_in"], np.float32).reshape(1, 80)
    cw1 = np.asarray(inputs["L0_attn_conv_w"], np.float32).reshape(D, 4)
    cw1 = np.ascontiguousarray(cw1.reshape(8, 128, 4).transpose(1, 0, 2)
                               .reshape(128, 32))
    cb1 = np.ascontiguousarray(
        np.asarray(inputs["L0_attn_conv_b"], np.float32).reshape(8, 128).T)
    woa = np.ascontiguousarray(np.concatenate(
        [np.asarray(inputs["L0_attn_Wout"], np.float32),
         np.asarray(inputs["L0_attn_b_out"], np.float32)[None, :]], axis=0))
    cw2 = np.asarray(inputs["L0_ffn_conv_w"], np.float32).reshape(HD, 3)
    cw2 = np.ascontiguousarray(cw2.reshape(32, 128, 3).transpose(1, 0, 2)
                               .reshape(128, 96))
    cb2 = np.ascontiguousarray(
        np.asarray(inputs["L0_ffn_conv_b"], np.float32).reshape(32, 128).T)
    alog = np.ascontiguousarray(
        np.asarray(inputs["L0_attn_A_log"], np.float32).reshape(16, 1))

    identf = np.eye(128, dtype=np.float32)
    onef = np.ones((1, 128), np.float32)
    onec = np.ones((128, 1), np.float32)

    in_maps = []
    for c in range(NCORES):
        s, j = c // 4, c % 4
        xs = np.zeros((XROWS, D), np.float32)
        lo = j * NT - 2
        a0, a1 = max(0, lo), min(16384, lo + XROWS)
        xs[a0 - lo:a1 - lo] = x[s, a0:a1]
        cm = np.zeros((16, 8), np.float32)
        for r in range(8):
            if r // 4 == s and r % 4 < j:
                cm[:, r] = 1.0
        in_maps.append(dict(
            x=xs, w1=w1, w2=w2, win=win, bin=binp, cw1=cw1, cb1=cb1,
            woa=woa, cw2=cw2, cb2=cb2, alog=alog,
            idf=identf, onef=onef, onec=onec,
            onesr=np.full((1, 2049), np.frombuffer(
                np.array([0x3F803F80], np.uint32).tobytes(),
                np.float32)[0], np.float32),
            constv=np.tile(np.array([[EPS, 1.0, RND32, -0.5, -1.0, 0.0,
                                      R16, -R16, -RND32]], np.float32),
                           (128, 1)),
            mconv=np.full((128, 1), 0.0 if j == 3 else 1.0, np.float32),
            mleft=np.full((128, 1), 0.0 if j == 0 else 1.0, np.float32),
            mright=np.full((128, 1), 0.0 if j == 3 else 1.0, np.float32),
            cmask=cm))
    return in_maps


_NC_CACHE = {}


def kernel(**inputs):
    in_maps = _stage_inputs(inputs)
    if "nc" not in _NC_CACHE:
        _NC_CACHE["nc"] = build_nc(debug=False)
    nc = _NC_CACHE["nc"]
    res = run_bass_kernel_spmd(nc, in_maps, core_ids=list(range(NCORES)))
    out = np.empty((2, 16384, D), np.float32)
    for c in range(NCORES):
        s, j = c // 4, c % 4
        out[s, j * NT:(j + 1) * NT] = res.results[c]["y_out"]
    return out


# version 26 (final)
# speedup vs baseline: 1.0966x; 1.0966x over previous
import contextlib
import numpy as np
import ml_dtypes
import concourse.bass as bass
import concourse.mybir as mybir
from concourse.bass_utils import run_bass_kernel_spmd

F32 = mybir.dt.float32
BF16 = mybir.dt.bfloat16
F16 = mybir.dt.float16
AF = mybir.ActivationFunctionType
OP = mybir.AluOpType
AX = mybir.AxisListType

NCORES = 8
D = 1024
HD = 4096
NT = 4096
NTILE = 16
NFRONT = 17
XROWS = 4356
RND32 = 1.5 * 2.0 ** 23
R16 = 1536.0
EPS = 1e-5
PS = 260
ACLIP = 2.5e-3

ENGS = ("sync", "vector", "scalar", "tensor", "gpsimd")


class B:

    def __init__(self, nc):
        self.nc = nc
        self.frozen = False
        self.ops = {e: [] for e in ENGS}
        self.cnt = {}
        self.sem_names = []
        for s in ("dx", "dw", "dw2", "dsm", "dst", "dmi", "ddb",
                  "sv", "ss", "sp", "sg", "scc"):
            self.cnt[s] = 0
            self.sem_names.append(s)

    def freeze(self):
        self.frozen = True

    def op(self, eng, fn, inc=None, n=1):
        if self.frozen:
            return self.cnt[inc] if inc else None
        self.ops[eng].append(("op", fn, inc, n))
        if inc is not None:
            self.cnt[inc] += n
        return self.cnt[inc] if inc else None

    def dma(self, eng, out, in_, inc):
        return self.op(eng, lambda e, o=out, i=in_: e.dma_start(out=o, in_=i),
                       inc, 16)

    def wait(self, eng, sem, val=None):
        if self.frozen:
            return
        v = self.cnt[sem] if val is None else val
        self.ops[eng].append(("wait", sem, v, None))

    def drain(self, eng):
        if self.frozen:
            return
        self.ops[eng].append(("drain", None, None, None))

    def bar(self, eng, sem, inc=1):
        if self.frozen:
            self.cnt[sem] += 0
            return self.cnt[sem]
        lst = self.ops[eng]
        if lst and lst[-1][0] == "op" and lst[-1][2] is None:
            kind, fn, _, _ = lst[-1]
            self.cnt[sem] += inc
            lst[-1] = ("op", fn, sem, inc)
            return self.cnt[sem]
        self.drain(eng)
        return self.op(eng, lambda e: e.nop(), sem, inc)

    def mark(self, sem):
        return self.cnt[sem]

    def replay(self, eng_name, eng, sems):
        for kind, a, b_, c in self.ops[eng_name]:
            if kind == "op":
                ins = a(eng)
                if b_ is not None:
                    ins.then_inc(sems[b_], c)
            elif kind == "wait":
                eng.wait_ge(sems[a], b_)
            else:
                eng.drain()


def build_nc(debug=False, phase=99, ntiles=NTILE):
    nc = bass.Bass("TRN2", target_bir_lowering=False)
    p = B(nc)

    def di(name, shape, dt=F32):
        return nc.dram_tensor(name, shape, dt, kind="ExternalInput").ap()

    dx = di("x", [XROWS, D])
    dw1 = di("w1", [HD, D])
    dw2 = di("w2", [D, HD])
    dwin = di("win", [D, 80])
    dbin = di("bin", [1, 80])
    dcw1 = di("cw1", [128, 32])
    dcb1 = di("cb1", [128, 8])
    dwoa = di("woa", [17, D])
    dcw2 = di("cw2", [128, 96])
    dcb2 = di("cb2", [128, 32])
    dalog = di("alog", [16, 1])
    didf = di("idf", [128, 128])
    donef = di("onef", [1, 128])
    donec = di("onec", [128, 1])
    dmconv = di("mconv", [128, 1])
    dmleft = di("mleft", [128, 1])
    dmright = di("mright", [128, 1])
    dcmask = di("cmask", [16, 8])
    dconstv = di("constv", [128, 9])
    donesr = di("onesr", [1, 2049])

    dout = nc.dram_tensor("y_out", [NT, D], F32, kind="ExternalOutput").ap()
    dbg = {}
    if debug:
        for nm, shp, dt in [("d_A", [16, 4098], F32),
                        ("d_C", [16, 2050], F32), ("d_h", [16, 4098], F32),
                        ("d_y0", [128, D // 2], F32), ("d_x2", [128, D // 2], F32),
                        ("d_P0", [128, 16 * PS], F32), ("d_Hq", [128, 8 * 256], F32),
                        ("d_scl", [1, 8], F32), ("d_xln", [128, D // 2], F32),
                        ("d_wink", [128, 1280], F32), ("d_At", [16, 512], F32),
                        ("d_wp", [128, 80], F32),
                        ("d_edge", [2, D // 2], F32)]:
            dbg[nm] = nc.dram_tensor(nm, shp, dt, kind="ExternalOutput").ap()

    cc_in = nc.dram_tensor("cc_in", [16, 2], F32)
    cc_out = nc.dram_tensor("cc_out", [128, 2], F32)

    ctx = contextlib.ExitStack()
    E = ctx.enter_context

    W1qT = E(nc.sbuf_tensor("W1qT", [128, 8 * HD], BF16))
    W2qT = E(nc.sbuf_tensor("W2qT", [128, 32 * D], F16))
    Ch = E(nc.sbuf_tensor("Ch", [17, 4098], BF16))
    WoaB = E(nc.sbuf_tensor("WoaB", [17, D], BF16))
    identb = E(nc.sbuf_tensor("s_identb", [128, 128], BF16))
    identh = E(nc.sbuf_tensor("s_identh", [128, 128], F16))
    onef = E(nc.sbuf_tensor("s_onef", [1, 128], F32))
    onec = E(nc.sbuf_tensor("s_onec", [128, 1], F32))
    oneh = E(nc.sbuf_tensor("s_oneh", [1, 128], F16))
    cw1 = E(nc.sbuf_tensor("s_cw1", [128, 32], F32))
    cb1 = E(nc.sbuf_tensor("s_cb1", [128, 8], F32))
    cw2 = E(nc.sbuf_tensor("s_cw2", [128, 96], F32))
    cb2 = E(nc.sbuf_tensor("s_cb2", [128, 32], F32))
    mconv = E(nc.sbuf_tensor("s_mconv", [128, 1], F32))
    mleft = E(nc.sbuf_tensor("s_mleft", [128, 1], F32))
    mright = E(nc.sbuf_tensor("s_mright", [128, 1], F32))
    cmask = E(nc.sbuf_tensor("s_cmask", [16, 8], F32))
    mhalf = E(nc.sbuf_tensor("mhalf", [128, 2], F32))
    scl = E(nc.sbuf_tensor("scl", [128, 8], F32))
    sclr = E(nc.sbuf_tensor("sclr", [1, 8], F32))
    constv = E(nc.sbuf_tensor("s_constv", [128, 9], F32))
    Pedge = E(nc.sbuf_tensor("Pedge", [128, 64], F16))
    cw2h = E(nc.sbuf_tensor("s_cw2h", [128, 96], F16))

    p_ya = E(nc.psum_tensor("p_ya", [128, 512], F32))
    p_tr = E(nc.psum_tensor("p_tr", [128, 512], F32))
    p_m1 = [E(nc.psum_tensor(f"p_m1{k}", [128, 512], F32)) for k in range(2)]
    p_m2 = [E(nc.psum_tensor(f"p_m2{k}", [128, 512], F32)) for k in range(4)]

    for o, i in [(onef[:, :], donef),
                 (onec[:, :], donec),
                 (cw1[:, :], dcw1), (cb1[:, :], dcb1),
                 (cw2[:, :], dcw2), (cb2[:, :], dcb2),
                 (mconv[:, :], dmconv), (mleft[:, :], dmleft),
                 (mright[:, :], dmright), (cmask[:, :], dcmask),
                 (constv[:, :], dconstv)]:
        p.dma("sync", o, i, "dsm")
    p.wait("vector", "dsm", p.mark("dsm"))
    p.op("vector", lambda v: v.tensor_copy(oneh[:, :], onef[:, :]))
    p.op("vector", lambda v: v.tensor_copy(cw2h[:, :], cw2[:, :]))
    p.bar("vector", "sv")
    consts_cast_sv = p.mark("sv")
    for k, val in enumerate([EPS, 1.0, RND32, -0.5, -1.0, 0.0, R16, -R16,
                             -RND32]):
        nc.const_aps.register(F32, val, constv[:, k:k + 1])
    consts_dsm = p.mark("dsm")
    p.op("gpsimd", lambda g: g.memset(mhalf[:, :], -0.5))
    p.bar("gpsimd", "sg")
    mhalf_sg = p.mark("sg")

    wctx = contextlib.ExitStack()
    EW = wctx.enter_context
    wsts = [EW(nc.sbuf_tensor(f"wst{b}", [128, 4096], F32)) for b in range(2)]
    terns = [EW(nc.sbuf_tensor(f"tern{b}", [128, 4096], BF16))
             for b in range(2)]
    wst = wsts[0]
    wsum = EW(nc.sbuf_tensor("wsum", [128, 16], F32))
    wcol = EW(nc.sbuf_tensor("wcol", [128, 2], F32))
    wsem = ["dw", "dw2"]

    p.wait("sync", "dsm", 0)
    p.dma("sync", wsts[0][:, 0:128], didf, "dw")
    p.wait("vector", "dw", p.mark("dw"))
    p.op("vector", lambda v: v.tensor_copy(identb[:, :], wsts[0][:, 0:128]))
    p.op("vector", lambda v: v.tensor_copy(identh[:, :], wsts[0][:, 0:128]))
    ident_sv = p.bar("vector", "sv")

    def w_src(wi, g, b):
        if wi == 0:
            return (dw1[512 * g:512 * (g + 1), :]
                    .rearrange("(s q) d -> q s d", s=4),
                    wsts[b][:, :].rearrange("q (s d) -> q s d", s=4))
        return dw2[128 * g:128 * (g + 1), :], wsts[b][:, :]

    wgs = [(wi, g) for wi in range(2) for g in range(8)]
    red_done = []
    dmamk = []

    def p1_red(idx):
        wi, g = wgs[idx]
        sem, mk = dmamk[idx]
        p.wait("vector", sem, mk)
        p.op("vector", lambda v, c=8 * wi + g, b=idx % 2: v.tensor_reduce(
            wsum[:, c:c + 1], wsts[b][:, :], AX.X, OP.add,
            apply_absolute_value=True))
        red_done.append(p.bar("vector", "sv"))

    for idx, (wi, g) in enumerate(wgs):
        b = idx % 2
        src, dst = w_src(wi, g, b)
        if idx == 0:
            p.wait("sync", "sv", ident_sv)
        if idx >= 2:
            p.wait("sync", "sv", red_done[idx - 2])
        p.dma("sync", dst, src, wsem[b])
        dmamk.append((wsem[b], p.mark(wsem[b])))
        if idx >= 1:
            p1_red(idx - 1)
    p1_red(15)
    p.op("vector", lambda v: v.tensor_reduce(
        wcol[:, 0:1], wsum[:, 0:8], AX.X, OP.add))
    p.op("vector", lambda v: v.tensor_reduce(
        wcol[:, 1:2], wsum[:, 8:16], AX.X, OP.add))
    p.bar("vector", "sv")
    p.wait("tensor", "sv", p.mark("sv"))
    p.wait("tensor", "dsm", consts_dsm)
    for k in range(2):
        p.op("tensor", lambda t, k=k: t.matmul(
            p_ya[0:1, k:k + 1], wcol[:, k:k + 1], onec[:, 0:1],
            start=True, stop=True))
    p.bar("tensor", "sp")
    p.wait("vector", "sp", p.mark("sp"))
    p.op("vector", lambda v: v.tensor_scalar(
        sclr[:, 0:1], p_ya[0:1, 0:1], 1.0 / (HD * D), 1e-5, OP.mult, OP.max))
    p.op("vector", lambda v: v.tensor_scalar(
        sclr[:, 1:2], p_ya[0:1, 1:2], 1.0 / (HD * D), 1e-5, OP.mult, OP.max))
    p.drain("vector")
    p.op("vector", lambda v: v.reciprocal(sclr[:, 2:3], sclr[:, 0:1]))
    p.op("vector", lambda v: v.reciprocal(sclr[:, 3:4], sclr[:, 1:2]))
    p.drain("vector")
    p.op("vector", lambda v: v.tensor_scalar(
        sclr[:, 4:5], sclr[:, 0:1], 1.0 / 127.0, None, OP.mult))
    p.op("vector", lambda v: v.tensor_scalar(
        sclr[:, 5:6], sclr[:, 1:2], 1.0 / 127.0, None, OP.mult))
    p.bar("vector", "sv")
    p.wait("tensor", "sv", p.mark("sv"))
    for k, c in [(2, 0), (3, 1), (4, 2), (5, 3)]:
        p.op("tensor", lambda t, k=k, c=c: t.matmul(
            p_tr[:, c:c + 1], onef[:, :], sclr[:, k:k + 1],
            start=True, stop=True))
    p.bar("tensor", "sp")
    p.wait("vector", "sp", p.mark("sp"))
    for c in range(4):
        p.op("vector", lambda v, c=c: v.tensor_copy(
            scl[:, c:c + 1], p_tr[:, c:c + 1]))
    mean_sv = p.bar("vector", "sv")
    if debug:
        p.wait("sync", "sv", p.mark("sv"))
        p.dma("sync", dbg["d_scl"], sclr[:, :], "ddb")

    ev_marks = []
    tern_built = []
    p1mk = dmamk

    def p2_body(idx):
        wi, g = wgs[idx]
        b = idx % 2
        sem, mk = dmamk[order.index(idx)]
        p.wait("scalar", sem, mk)
        p.wait("scalar", "sv", mean_sv)
        p.op("scalar", lambda s, c=wi, b=b: s.activation(
            wsts[b][:, :], wsts[b][:, :], AF.Copy, bias=RND32,
            scale=scl[:, c:c + 1]))
        p.op("scalar", lambda s, b=b: s.activation(
            wsts[b][:, :], wsts[b][:, :], AF.Copy, bias=-RND32))
        p.bar("scalar", "ss")
        p.wait("vector", "ss", p.mark("ss"))
        p.op("vector", lambda v, b=b: v.tensor_scalar(
            terns[b][:, :], wsts[b][:, :], 1.0, -1.0, OP.min, OP.max))
        tern_built.append(p.bar("vector", "sv"))
        p.wait("tensor", "sv", tern_built[-1])
        for bt in range(8):
            pb = p_tr if bt % 2 == 0 else p_m1[0]
            if len(ev_marks) >= 2:
                p.wait("tensor", "sv", ev_marks[-2])
            for q in range(4):
                ti = 4 * bt + q
                if wi == 0:
                    s4, dc = ti // 8, ti % 8
                    lhsT = terns[b][:, 1024 * s4 + 128 * dc:
                                    1024 * s4 + 128 * (dc + 1)]
                else:
                    lhsT = terns[b][:, 128 * ti:128 * (ti + 1)]
                idt = identb[:, :]
                p.op("tensor", lambda t, pb=pb, q=q, l=lhsT, idt=idt:
                     t.matmul(pb[:, 128 * q:128 * (q + 1)], l, idt,
                              start=True, stop=True))
            mmk = p.bar("tensor", "sp")
            p.wait("vector", "sp", mmk)
            for q in range(4):
                ti = 4 * bt + q
                if wi == 0:
                    s4, dc = ti // 8, ti % 8
                    s = 4 * g + s4
                    dd = W1qT[:, HD * dc + 128 * s:HD * dc + 128 * (s + 1)]
                else:
                    dd = W2qT[:, D * ti + 128 * g:D * ti + 128 * (g + 1)]
                p.op("vector", lambda v, pb=pb, q=q, dd=dd:
                     v.tensor_copy(dd, pb[:, 128 * q:128 * (q + 1)]))
            ev_marks.append(p.bar("vector", "sv"))

    order = [14, 15] + list(range(14))
    dmamk = []
    for pos, idx in enumerate(order):
        wi, g = wgs[idx]
        b = pos % 2
        assert idx % 2 == b
        if pos < 2:
            dmamk.append(p1mk[idx])
        else:
            p.wait("sync", "sv", tern_built[pos - 2])
            src, dst = w_src(wi, g, b)
            p.dma("sync", dst, src, wsem[b])
            dmamk.append((wsem[b], p.mark(wsem[b])))
        if pos >= 1:
            p2_body(order[pos - 1])
    p2_body(order[15])
    wq_sv = p.mark("sv")
    if phase < 2:
        p.freeze()

    wctx.close()
    actx = contextlib.ExitStack()
    EA = actx.enter_context
    hbuf = EA(nc.sbuf_tensor("hbuf", [16, 4098], F32))
    acum = EA(nc.sbuf_tensor("acum", [16, 4098], F32))
    Cp = EA(nc.sbuf_tensor("Cp", [16, 4100], BF16))
    xlnfm = EA(nc.sbuf_tensor("xlnfm", [128, 8 * PS], BF16))
    xa = EA(nc.sbuf_tensor("xa", [128, D], F32))
    xa2 = EA(nc.sbuf_tensor("xa2", [128, D], F32))
    xlna = EA(nc.sbuf_tensor("xlna", [128, 2 * D], BF16))
    sp16 = EA(nc.sbuf_tensor("sp16", [16, 5 * 256], F32))
    wpad = xa[:, 128:208]
    wink = EA(nc.sbuf_tensor("wink", [128, 32 * 80], BF16))
    biasc = Pedge[:, :].bitcast(F32)[0:80, 0:4]
    a16 = EA(nc.sbuf_tensor("a16", [16, 34], F32))
    biasc0 = a16[:, 31:34]
    rowb = sp16[0:1, 0:336]
    acol = a16[:, 0:12]
    dAacc = a16[:, 12:29]

    ccin_sb = a16[:, 29:31]
    stat = EA(nc.sbuf_tensor("stat", [128, 28], F32))
    gath = stat[0:16, 0:16]

    p.dma("sync", acol[:, 3:4], dalog, "dsm")
    p.wait("scalar", "dsm", p.mark("dsm"))
    p.op("scalar", lambda s: s.activation(acol[:, 3:4], acol[:, 3:4], AF.Exp))
    p.bar("scalar", "ss")
    p.wait("vector", "ss", p.mark("ss"))
    p.op("vector", lambda v: v.tensor_scalar(
        acol[:, 1:2], acol[:, 3:4], 1e-4, None, OP.is_lt))
    p.drain("vector")
    p.op("vector", lambda v: v.tensor_scalar(
        acol[:, 0:1], acol[:, 3:4], -1.0, None, OP.mult))
    p.drain("vector")
    p.op("vector", lambda v: v.tensor_scalar(
        acol[:, 5:6], acol[:, 0:1], -1.0, 1.0, OP.mult, OP.add))
    p.drain("vector")
    p.op("vector", lambda v: v.scalar_tensor_tensor(
        acol[:, 5:6], acol[:, 5:6], acol[:, 1:2], acol[:, 0:1],
        OP.mult, OP.add))
    p.drain("vector")
    p.op("vector", lambda v: v.reciprocal(acol[:, 2:3], acol[:, 5:6]))
    p.drain("vector")

    def avg_pairs(dst, src):
        p.op("vector", lambda v, d=dst, s=src: v.tensor_tensor(
            d, s.rearrange("q (a b) -> q a b", b=2)[:, :, 0:1],
            s.rearrange("q (a b) -> q a b", b=2)[:, :, 1:2], OP.add))

    p.wait("vector", "dsm", consts_dsm)
    p.op("vector", lambda v: v.memset(wpad[:, :], 0.0))
    p.drain("vector")
    for dc in range(8):
        p.wait("sync", "sv", p.mark("sv"))
        p.dma("sync", xa[:, 0:80], dwin[128 * dc:128 * (dc + 1), :], "dsm")
        p.wait("vector", "dsm", p.mark("dsm"))
        p.op("vector", lambda v: v.tensor_copy(wpad[:, 0:16], xa[:, 0:16]))
        avg_pairs(wpad[:, 32:48], xa[:, 16:48])
        avg_pairs(wpad[:, 64:80], xa[:, 48:80])
        p.drain("vector")
        p.op("vector", lambda v: v.tensor_scalar(
            wpad[:, 32:48], wpad[:, 32:48], 0.5, None, OP.mult))
        p.op("vector", lambda v: v.tensor_scalar(
            wpad[:, 64:80], wpad[:, 64:80], 0.5, None, OP.mult))
        p.drain("vector")
        for k in range(4):
            p.op("vector", lambda v, dc=dc, k=k: v.tensor_scalar(
                wink[:, 80 * (4 * dc + k):80 * (4 * dc + k + 1)], wpad[:, 0:80],
                cw1[:, 4 * dc + k:4 * dc + k + 1], None, OP.mult))
        wk_sv = p.bar("vector", "sv")
        p.wait("tensor", "sv", wk_sv)
        p.op("tensor", lambda t, dc=dc: t.matmul(
            p_ya[0:1, 8:88], cb1[:, dc:dc + 1], wpad[:, 0:80],
            start=(dc == 0), stop=(dc == 7)))
    p.bar("tensor", "sp")
    p.wait("sync", "sv", p.mark("sv"))
    p.dma("sync", rowb[0:1, 256:336], dbin, "dsm")
    p.wait("vector", "dsm", p.mark("dsm"))
    p.op("vector", lambda v: v.memset(rowb[0:1, 0:256], 0.0))
    p.drain("vector")
    p.op("vector", lambda v: v.tensor_copy(rowb[0:1, 0:16], rowb[0:1, 256:272]))
    avg_pairs(rowb[0:1, 32:48], rowb[0:1, 272:304])
    avg_pairs(rowb[0:1, 64:80], rowb[0:1, 304:336])
    p.drain("vector")
    p.op("vector", lambda v: v.tensor_scalar(
        rowb[0:1, 32:48], rowb[0:1, 32:48], 0.5, None, OP.mult))
    p.op("vector", lambda v: v.tensor_scalar(
        rowb[0:1, 64:80], rowb[0:1, 64:80], 0.5, None, OP.mult))
    p.drain("vector")
    p.wait("vector", "sp", p.mark("sp"))
    p.op("vector", lambda v: v.tensor_tensor(
        rowb[0:1, 80:160], p_ya[0:1, 8:88], rowb[0:1, 0:80], OP.add))
    p.bar("vector", "sv")
    p.wait("tensor", "sv", p.mark("sv"))
    p.op("tensor", lambda t: t.matmul(
        p_tr[0:80, 0:1], rowb[0:1, 80:160], onef[0:1, 0:1],
        start=True, stop=True))
    p.op("tensor", lambda t: t.matmul(
        p_tr[0:80, 1:2], rowb[0:1, 0:80], onef[0:1, 0:1],
        start=True, stop=True))
    for k, off in ((0, 80), (1, 112), (2, 144)):
        p.op("tensor", lambda t, k=k, off=off: t.matmul(
            p_tr[0:16, 4 + k:5 + k], rowb[0:1, off:off + 16], onef[0:1, 0:1],
            start=True, stop=True))
    p.bar("tensor", "sp")
    p.wait("vector", "sp", p.mark("sp"))
    p.op("vector", lambda v: v.tensor_copy(biasc[:, 0:1], p_tr[0:80, 0:1]))
    p.op("vector", lambda v: v.tensor_copy(biasc[:, 1:2], p_tr[0:80, 1:2]))
    for k in range(3):
        p.op("vector", lambda v, k=k: v.tensor_copy(
            biasc0[:, k:k + 1], p_tr[0:16, 4 + k:5 + k]))
    p.drain("vector")
    p.op("vector", lambda v: v.tensor_tensor(
        biasc[:, 2:3], biasc[:, 1:2], biasc[:, 0:1], OP.subtract))
    p.drain("vector")
    p.op("vector", lambda v: v.tensor_scalar(
        biasc[:, 3:4], biasc[:, 2:3], -1.0, None, OP.mult))
    p.drain("vector")
    p.wait("sync", "sv", p.mark("sv"))
    p.dma("sync", xa[0:17, 0:D], dwoa, "dsm")
    p.wait("vector", "dsm", p.mark("dsm"))
    p.op("vector", lambda v: v.tensor_copy(WoaB[:, :], xa[0:17, 0:D]))
    consts_sv = p.bar("vector", "sv")
    if debug:
        p.wait("sync", "sv", consts_sv)
        p.dma("sync", dbg["d_wink"], wink[:, :].bitcast(F32), "ddb")
        p.dma("sync", dbg["d_wp"], wpad[:, :], "ddb")

    def ln1(xsl, dst):
        p.op("vector", lambda v: v.bn_stats(stat[:, 0:6], xsl[:, 0:512]))
        p.op("vector", lambda v: v.bn_stats(stat[:, 6:12], xsl[:, 512:1024]))
        p.drain("vector")
        p.op("vector", lambda v: v.bn_aggr(stat[:, 12:14], stat[:, 0:12]))
        p.drain("vector")
        sv = p.bar("vector", "sv")
        p.wait("scalar", "sv", sv)
        p.op("scalar", lambda s: s.activation(
            stat[:, 14:15], stat[:, 13:14], AF.Ln, bias=EPS))
        p.drain("scalar")
        p.op("scalar", lambda s: s.activation(
            stat[:, 15:16], stat[:, 14:15], AF.Exp, scale=-0.5))
        ssm = p.bar("scalar", "ss")
        p.wait("vector", "ss", ssm)
        p.op("vector", lambda v: v.tensor_scalar(
            stat[:, 16:17], stat[:, 12:13], stat[:, 15:16], -1.0,
            OP.mult, OP.mult))
        p.op("vector", lambda v: v.tensor_scalar(
            dst, xsl, stat[:, 15:16], stat[:, 16:17], OP.mult, OP.add))
        return p.bar("vector", "sv")

    def transpose_batch(srcs, dsts, ident, prev_ev):
        if prev_ev is not None:
            p.wait("tensor", "sv", prev_ev)
        for q, s in enumerate(srcs):
            p.op("tensor", lambda t, q=q, s=s: t.matmul(
                p_tr[:, 128 * q:128 * (q + 1)], s, ident,
                start=True, stop=True))
        mk = p.bar("tensor", "sp")
        p.wait("vector", "sp", mk)
        for q, dd in enumerate(dsts):
            p.op("vector", lambda v, q=q, dd=dd: v.tensor_copy(
                dd, p_tr[:, 128 * q:128 * (q + 1)]))
        return p.bar("vector", "sv")

    p.wait("sync", "sv", p.mark("sv"))
    p.dma("sync", xa[:, :], dx[0:128, :], "dx")
    p.wait("vector", "dx", p.mark("dx"))
    p.wait("vector", "sv", consts_sv)
    pre_ss = ln1(xa[:, :], xlna[:, 0:D])
    p.wait("tensor", "sv", pre_ss)
    ev = None
    for half in range(2):
        srcs = [xlna[:, 128 * (4 * half + q):128 * (4 * half + q + 1)]
                for q in range(4)]
        dsts = [xlnfm[:, PS * (4 * half + q) + 2:PS * (4 * half + q) + 4]
                for q in range(4)]
        if ev is not None:
            p.wait("tensor", "sv", ev)
        for q, s in enumerate(srcs):
            p.op("tensor", lambda t, q=q, s=s: t.matmul(
                p_tr[:, 128 * q:128 * (q + 1)], s, identb[:, :],
                start=True, stop=True))
        mk = p.bar("tensor", "sp")
        p.wait("vector", "sp", mk)
        for q, dd in enumerate(dsts):
            p.op("vector", lambda v, q=q, dd=dd: v.tensor_copy(
                dd, p_tr[:, 128 * q:128 * q + 2]))
        ev = p.bar("vector", "sv")
    p.op("vector", lambda v: v.memset(
        xlnfm[:, :].rearrange("q (c w) -> q c w", w=PS)[:, :, 0:1], 0.0))
    p.bar("vector", "sv")

    ep_done = {}
    mm_mk = {}
    xl_mk = {}

    def ln1_pair(c):
        p.wait("sync", "sv", p.mark("sv"))
        row0 = 256 * c + 2
        p.dma("sync", xa[:, :], dx[row0:row0 + 128, :], "dx")
        p.dma("sync", xa2[:, :], dx[row0 + 128:row0 + 256, :], "dx")
        p.wait("vector", "dx", p.mark("dx"))
        p.wait("vector", "sv", consts_sv)
        p.op("vector", lambda v: v.bn_stats(stat[:, 0:6], xa[:, 0:512]))
        p.op("vector", lambda v: v.bn_stats(stat[:, 6:12], xa[:, 512:1024]))
        p.op("vector", lambda v: v.bn_stats(stat[:, 12:18], xa2[:, 0:512]))
        p.op("vector", lambda v: v.bn_stats(stat[:, 18:24], xa2[:, 512:1024]))
        p.drain("vector")
        p.op("vector", lambda v: v.bn_aggr(stat[:, 24:26], stat[:, 0:12]))
        p.op("vector", lambda v: v.bn_aggr(stat[:, 26:28], stat[:, 12:24]))
        sv = p.bar("vector", "sv")
        p.wait("scalar", "sv", sv)
        p.op("scalar", lambda s: s.activation(
            stat[:, 0:4], stat[:, 24:28], AF.Ln, bias=EPS))
        p.drain("scalar")
        p.op("scalar", lambda s: s.activation(
            stat[:, 4:8], stat[:, 0:4], AF.Exp, scale=-0.5))
        ssm = p.bar("scalar", "ss")
        p.wait("vector", "ss", ssm)
        for sb in range(2):
            p.op("vector", lambda v, sb=sb: v.tensor_scalar(
                stat[:, 8 + sb:9 + sb], stat[:, 24 + 2 * sb:25 + 2 * sb],
                stat[:, 5 + 2 * sb:6 + 2 * sb], -1.0, OP.mult, OP.mult))
        p.drain("vector")
        for sb, xsl in enumerate([xa, xa2]):
            p.op("vector", lambda v, xsl=xsl, sb=sb: v.tensor_scalar(
                xlna[:, D * sb:D * (sb + 1)], xsl[:, :],
                stat[:, 5 + 2 * sb:6 + 2 * sb],
                stat[:, 8 + sb:9 + sb], OP.mult, OP.add))
        mk = p.bar("vector", "sv")
        if debug and c == 0:
            p.wait("sync", "sv", mk)
            p.dma("sync", dbg["d_xln"], xlna[:, 0:D].bitcast(F32), "ddb")
        return mk

    def stage1(c):
        ss_ln = ln1_pair(c)
        if c > 0:
            p.wait("vector", "sp", mm_mk[c - 1])
            p.op("vector", lambda v: v.tensor_copy(
                xlnfm[:, :].rearrange("q (cc w) -> q cc w", w=PS)[:, :, 1:4],
                xlnfm[:, :].rearrange("q (cc w) -> q cc w", w=PS)[:, :,
                                                                  257:260]))
            p.bar("vector", "sv")
        p.wait("tensor", "sv", ss_ln)
        ev = None
        for s_ in range(2):
            for half in range(2):
                srcs = [xlna[:, D * s_ + 128 * (4 * half + q):
                             D * s_ + 128 * (4 * half + q + 1)]
                        for q in range(4)]
                dsts = [xlnfm[:, PS * (4 * half + q) + 4 + 128 * s_:
                              PS * (4 * half + q) + 4 + 128 * (s_ + 1)]
                        for q in range(4)]
                ev = transpose_batch(srcs, dsts, identb[:, :], ev)
        pp = p_m1[c % 2]
        p.wait("tensor", "sv", ev)
        if c >= 2:
            p.wait("tensor", "sv", ep_done[c - 2])
        for dc in range(8):
            for k in range(4):
                p.op("tensor", lambda t, dc=dc, k=k, pp=pp: t.matmul(
                    pp[0:80, 0:256],
                    wink[:, 80 * (4 * dc + k):80 * (4 * dc + k + 1)],
                    xlnfm[:, PS * dc + 1 + k:PS * dc + 257 + k],
                    start=(dc == 0 and k == 0), stop=(dc == 7 and k == 3)))
        mm_mk[c] = p.bar("tensor", "sp")

    def stage2(c):
        pp = p_m1[c % 2]
        mm = mm_mk[c]
        col0 = 256 * c
        W = 256 if c < NFRONT - 1 else 4
        sl = [sp16[:, 256 * k:256 * (k + 1)] for k in range(5)]
        e = [sl[0], sl[1], sl[2], sl[1], sl[0], sl[1], sl[2]]
        pa = sl[3][:, 0:W]
        pb_ = sl[4][:, 0:W]
        pc_ = Cp[0:16, col0:col0 + W]
        if c == NFRONT - 1:
            p.wait("vector", "sp", mm)
            p.op("vector", lambda v, pp=pp: v.tensor_scalar(
                pp[0:80, 1:2], pp[0:80, 1:2], biasc[:, 3:4],
                mconv[0:80, 0:1], OP.add, OP.mult))
            p.drain("vector")
            p.op("vector", lambda v, pp=pp: v.tensor_tensor(
                pp[0:80, 1:2], pp[0:80, 1:2], biasc[0:80, 2:3], OP.add))
            mm = p.bar("vector", "sv")
            p.wait("scalar", "sv", mm)
        else:
            p.wait("scalar", "sp", mm)
        p.op("scalar", lambda s, pp=pp: s.activation(
            e[0], pp[0:16, 0:256], AF.Abs, bias=biasc0[:, 0:1]))
        p.op("scalar", lambda s, pp=pp: s.activation(
            e[2], pp[0:16, 0:256], AF.Relu, bias=biasc0[:, 0:1]))
        p.drain("scalar")
        p.op("scalar", lambda s: s.activation(e[1], e[0], AF.Exp, scale=-1.0))
        p.drain("scalar")
        p.op("scalar", lambda s: s.activation(e[1], e[1], AF.Ln, bias=1.0))
        sm = p.bar("scalar", "ss")
        p.wait("vector", "ss", sm)
        p.op("vector", lambda v: v.tensor_tensor(e[3], e[1], e[2], OP.add))
        p.drain("vector")
        p.op("vector", lambda v: v.tensor_scalar(
            e[4], e[3], acol[:, 0:1], None, OP.mult))
        p.drain("vector")
        rsl = e[4][:, 2:256] if c == 0 else (
            e[4][:, 0:2] if c == NFRONT - 1 else e[4])
        p.op("vector", lambda v, r=rsl, c=c: v.tensor_reduce(
            dAacc[:, c:c + 1], r, AX.X, OP.add))
        sv = p.bar("vector", "sv")
        p.wait("scalar", "sv", sv)
        p.op("scalar", lambda s, pa=pa, W=W: s.activation(
            pa, e[4][:, 0:W], AF.Exp))
        sm = p.bar("scalar", "ss")
        p.wait("vector", "ss", sm)
        p.op("vector", lambda v: v.tensor_scalar(
            e[5], e[4], 1.0 / 6.0, 0.5, OP.mult, OP.add))
        p.drain("vector")
        p.op("vector", lambda v: v.tensor_tensor(e[5], e[5], e[4], OP.mult))
        p.op("vector", lambda v, pa=pa, W=W: v.tensor_scalar(
            e[6][:, 0:W], pa, -1.0, acol[:, 2:3], OP.add, OP.mult))
        p.drain("vector")
        p.op("vector", lambda v: v.tensor_scalar(e[5], e[5], 1.0, None,
                                                 OP.add))
        p.drain("vector")
        p.op("vector", lambda v: v.tensor_tensor(e[5], e[5], e[6],
                                                 OP.subtract))
        p.drain("vector")
        p.op("vector", lambda v: v.scalar_tensor_tensor(
            e[5], e[5], acol[:, 1:2], e[6], OP.mult, OP.add))
        p.drain("vector")
        p.op("vector", lambda v, pp=pp, d_=pb_, W=W: v.scalar_tensor_tensor(
            d_, pp[32:48, 0:W], biasc0[:, 1:2], e[5][:, 0:W],
            OP.add, OP.mult))
        p.op("vector", lambda v, pp=pp, d_=pc_, W=W: v.tensor_scalar(
            d_, pp[64:80, 0:W], biasc0[:, 2:3], None, OP.add))
        ep_done[c] = p.bar("vector", "sv")
        if debug and c == 1:
            p.wait("sync", "sv", ep_done[c])
            p.dma("sync", dbg["d_At"], sp16[:, 3 * 256:5 * 256], "ddb")
            p.wait("scalar", "ddb", p.mark("ddb"))
            p.wait("vector", "ddb", p.mark("ddb"))
        if c > 0:
            p.op("vector", lambda v, c=c, sl=sl: v.tensor_tensor_scan(
                hbuf[:, 256 * c - 1:256 * c + 1], sl[3][:, 0:2],
                sl[4][:, 0:2], hbuf[:, 256 * c - 2:256 * c - 1],
                OP.mult, OP.add))
            p.op("vector", lambda v, c=c, sl=sl: v.tensor_tensor_scan(
                acum[:, 256 * c - 1:256 * c + 1], sl[3][:, 0:2],
                sl[3][:, 0:2], acum[:, 256 * c - 2:256 * c - 1],
                OP.mult, OP.bypass))
            p.drain("vector")
        if c < NFRONT - 1:
            if c == 0:
                p.op("vector", lambda v, sl=sl: v.tensor_tensor_scan(
                    hbuf[:, 1:255], sl[3][:, 2:256], sl[4][:, 2:256], 0.0,
                    OP.mult, OP.add))
                p.op("vector", lambda v, sl=sl: v.tensor_tensor_scan(
                    acum[:, 1:255], sl[3][:, 2:256], sl[3][:, 2:256], 1.0,
                    OP.mult, OP.bypass))
            else:
                p.op("vector", lambda v, c=c, sl=sl: v.tensor_tensor_scan(
                    hbuf[:, 256 * c + 1:256 * c + 255], sl[3][:, 2:256],
                    sl[4][:, 2:256], hbuf[:, 256 * c:256 * c + 1],
                    OP.mult, OP.add))
                p.op("vector", lambda v, c=c, sl=sl: v.tensor_tensor_scan(
                    acum[:, 256 * c + 1:256 * c + 255], sl[3][:, 2:256],
                    sl[3][:, 2:256], acum[:, 256 * c:256 * c + 1],
                    OP.mult, OP.bypass))
            p.drain("vector")
        else:
            p.op("vector", lambda v, sl=sl: v.tensor_copy(
                acol[:, 9:10], sl[3][:, 2:3]))
            p.op("vector", lambda v, sl=sl: v.tensor_copy(
                acol[:, 10:11], sl[4][:, 2:3]))
            p.drain("vector")
        ep_done[c] = p.bar("vector", "sv")

    for c in range(NFRONT):
        stage1(c)
        if c > 0:
            stage2(c - 1)
    stage2(NFRONT - 1)

    if phase < 3:
        p.freeze()
    p.op("vector", lambda v: v.tensor_reduce(
        acol[:, 6:7], dAacc[:, 0:NFRONT], AX.X, OP.add))
    sv = p.bar("vector", "sv")
    p.wait("scalar", "sv", sv)
    p.op("scalar", lambda s: s.activation(acol[:, 7:8], acol[:, 6:7], AF.Exp))
    sm = p.bar("scalar", "ss")
    p.wait("vector", "ss", sm)
    p.op("vector", lambda v: v.tensor_copy(ccin_sb[:, 0:1], acol[:, 7:8]))
    p.op("vector", lambda v: v.tensor_copy(ccin_sb[:, 1:2],
                                           hbuf[:, 4096:4097]))
    sv = p.bar("vector", "sv")
    p.wait("gpsimd", "sv", sv)
    p.dma("gpsimd", cc_in.ap(), ccin_sb[:, :], "dmi")
    p.wait("gpsimd", "dmi", p.mark("dmi"))
    p.op("gpsimd", lambda g: g.collective_compute(
        "AllGather", OP.bypass, replica_groups=[list(range(NCORES))],
        ins=[cc_in.ap()], outs=[cc_out.ap()]), "scc")
    p.wait("gpsimd", "scc", p.mark("scc"))
    p.dma("gpsimd", gath[:, :].rearrange("n (r k) -> n r k", k=2),
          cc_out.ap().rearrange("(r n) k -> n r k", r=8, n=16, k=2), "dmi")
    p.wait("vector", "dmi", p.mark("dmi"))
    p.op("vector", lambda v: v.memset(acol[:, 4:5], 0.0))
    p.drain("vector")
    for r in range(7):
        p.op("vector", lambda v, r=r: v.scalar_tensor_tensor(
            acol[:, 5:6], acol[:, 4:5], gath[:, 2 * r:2 * r + 1],
            gath[:, 2 * r + 1:2 * r + 2], OP.mult, OP.add))
        p.drain("vector")
        p.op("vector", lambda v: v.tensor_tensor(
            acol[:, 5:6], acol[:, 5:6], acol[:, 4:5], OP.subtract))
        p.drain("vector")
        p.op("vector", lambda v, r=r: v.scalar_tensor_tensor(
            acol[:, 4:5], acol[:, 5:6], cmask[:, r:r + 1], acol[:, 4:5],
            OP.mult, OP.add))
        p.drain("vector")
    p.op("vector", lambda v: v.scalar_tensor_tensor(
        hbuf[:, 1:4097], acum[:, 1:4097], acol[:, 4:5], hbuf[:, 1:4097],
        OP.mult, OP.add))
    p.drain("vector")
    p.op("vector", lambda v: v.tensor_copy(hbuf[:, 0:1], acol[:, 4:5]))
    p.drain("vector")
    p.op("vector", lambda v: v.scalar_tensor_tensor(
        hbuf[:, 4097:4098], hbuf[:, 4096:4097], acol[:, 9:10],
        acol[:, 10:11], OP.mult, OP.add))
    p.drain("vector")
    p.wait("sync", "sv", p.mark("sv"))
    p.dma("sync", Ch[16:17, :].bitcast(F32), donesr, "dsm")
    p.op("vector", lambda v: v.tensor_tensor(
        Ch[0:16, 0:1], Cp[0:16, 1:2], hbuf[:, 0:1], OP.mult))
    p.op("vector", lambda v: v.tensor_tensor(
        Ch[0:16, 1:2], Cp[0:16, 4098:4099], hbuf[:, 4097:4098], OP.mult))
    p.op("vector", lambda v: v.tensor_tensor(
        Ch[0:16, 2:4098], Cp[0:16, 2:4098], hbuf[:, 1:4097], OP.mult))
    p.drain("vector")
    p.op("vector", lambda v: v.nop(), "sv")
    p.wait("vector", "dsm", p.mark("dsm"))
    scan_sv = p.bar("vector", "sv")
    if debug:
        p.wait("sync", "sv", scan_sv)
        p.dma("sync", dbg["d_A"], acum[:, 0:4098], "ddb")
        p.dma("sync", dbg["d_C"], Cp[0:16, :].bitcast(F32), "ddb")
        p.dma("sync", dbg["d_h"], hbuf[:, :], "ddb")

    actx.close()

    cctx = contextlib.ExitStack()
    EC = cctx.enter_context
    P0 = EC(nc.sbuf_tensor("P0", [128, 32 * PS], F16))
    P1 = EC(nc.sbuf_tensor("P1", [128, 32 * PS], F16))
    H2 = EC(nc.sbuf_tensor("H2", [128, 16 * 256], F16))
    outt = H2[:, 2048:3072].bitcast(F32)
    outt2 = H2[:, 3072:4096].bitcast(F32)
    ysbB = H2[:, 0:2048].bitcast(BF16)

    def x2keep(t):
        return Pb[t % 2][:, 0:2048].bitcast(BF16)
    xcf = EC(nc.sbuf_tensor("xcf", [128, 2 * D], F32))
    x2t = [EC(nc.sbuf_tensor(f"x2_{k}", [128, 2 * D], BF16)) for k in range(2)]
    ysb = EC(nc.sbuf_tensor("ysb", [128, 2 * D], BF16))
    xqfm = EC(nc.sbuf_tensor("xqfm", [128, 8 * 256], BF16))
    qf = xqfm[:, :].bitcast(F16)
    s2b = EC(nc.sbuf_tensor("s2b", [128, 256], F16))
    cst = EC(nc.sbuf_tensor("cst", [128, 44], F32))
    s2h = EC(nc.sbuf_tensor("s2h", [128, 2], F16))
    srh = EC(nc.sbuf_tensor("srh", [1, 256], F16))
    xe = xcf[0:2, D:2 * D]
    ee = xcf[0:2, 0:D]
    Pb = [P0, P1]


    def ln_pow(parts):
        src, P = parts
        p.op("vector", lambda v: v.bn_stats(cst[0:P, 0:6], src[:, 0:512]))
        p.op("vector", lambda v: v.bn_stats(cst[0:P, 6:12], src[:, 512:1024]))
        p.drain("vector")
        p.op("vector", lambda v: v.bn_aggr(cst[0:P, 12:14], cst[0:P, 0:12]))
        p.drain("vector")
        p.op("vector", lambda v: v.tensor_scalar(
            cst[0:P, 14:15], cst[0:P, 13:14], EPS, None, OP.add))
        sv = p.bar("vector", "sv")
        p.wait("gpsimd", "sv", sv)
        p.op("gpsimd", lambda g: g.tensor_tensor(
            cst[0:P, 15:16], cst[0:P, 14:15], mhalf[0:P, 0:1], OP.pow))
        sg = p.bar("gpsimd", "sg")
        p.wait("vector", "sg", sg)
        p.op("vector", lambda v: v.tensor_scalar(
            cst[0:P, 16:17], cst[0:P, 12:13], cst[0:P, 15:16], -1.0,
            OP.mult, OP.mult))
        p.drain("vector")

    def ln_pair(src0, src1):
        p.op("vector", lambda v: v.bn_stats(cst[:, 0:6], src0[:, 0:512]))
        p.op("vector", lambda v: v.bn_stats(cst[:, 6:12], src0[:, 512:1024]))
        p.op("vector", lambda v: v.bn_stats(cst[:, 12:18], src1[:, 0:512]))
        p.op("vector", lambda v: v.bn_stats(cst[:, 18:24], src1[:, 512:1024]))
        p.drain("vector")
        p.op("vector", lambda v: v.bn_aggr(cst[:, 24:26], cst[:, 0:12]))
        p.op("vector", lambda v: v.bn_aggr(cst[:, 26:28], cst[:, 12:24]))
        p.drain("vector")
        mv = cst[:, 24:28].rearrange("q (a b) -> q a b", b=2)
        p.op("vector", lambda v, mv=mv: v.tensor_scalar(
            cst[:, 28:30].rearrange("q (a b) -> q a b", b=1),
            mv[:, :, 1:2], EPS, None, OP.add))
        sv = p.bar("vector", "sv")
        p.wait("gpsimd", "sv", sv)
        p.op("gpsimd", lambda g: g.tensor_tensor(
            cst[:, 30:32], cst[:, 28:30], mhalf[:, 0:2], OP.pow))
        sg = p.bar("gpsimd", "sg")
        p.wait("vector", "sg", sg)
        p.op("vector", lambda v, mv=mv: v.tensor_tensor(
            cst[:, 32:34].rearrange("q (a b) -> q a b", b=1),
            mv[:, :, 0:1],
            cst[:, 30:32].rearrange("q (a b) -> q a b", b=1), OP.mult))
        p.drain("vector")

    def front_chain2(i):
        x2b = x2t[i % 2]
        ln_pair(ysb[:, 0:D], ysb[:, D:2 * D])
        for sb in range(2):
            p.op("vector", lambda v, sb=sb, x2b=x2b: v.scalar_tensor_tensor(
                x2b[:, D * sb:D * (sb + 1)], ysb[:, D * sb:D * (sb + 1)],
                cst[:, 30 + sb:31 + sb], xcf[:, D * sb:D * (sb + 1)],
                OP.mult, OP.add))
        p.drain("vector")
        for sb in range(2):
            p.op("vector", lambda v, sb=sb, x2b=x2b: v.tensor_scalar(
                x2b[:, D * sb:D * (sb + 1)], x2b[:, D * sb:D * (sb + 1)],
                cst[:, 32 + sb:33 + sb], None, OP.subtract))
        p.drain("vector")
        ln_pair(x2b[:, 0:D], x2b[:, D:2 * D])
        for sb in range(2):
            p.op("vector", lambda v, sb=sb, x2b=x2b: v.tensor_scalar(
                ysb[:, D * sb:D * (sb + 1)], x2b[:, D * sb:D * (sb + 1)],
                cst[:, 30 + sb:31 + sb], cst[:, 32 + sb:33 + sb],
                OP.mult, OP.subtract))
        p.drain("vector")
        for sb in range(2):
            p.op("vector", lambda v, sb=sb: v.tensor_reduce(
                cst[:, 34 + sb:35 + sb], ysb[:, D * sb:D * (sb + 1)],
                AX.X, OP.max, apply_absolute_value=True))
        p.drain("vector")
        p.op("vector", lambda v: v.tensor_scalar(
            cst[:, 34:36], cst[:, 34:36], 1e-5, None, OP.max))
        p.drain("vector")
        p.op("vector", lambda v: v.reciprocal(cst[:, 36:38], cst[:, 34:36]))
        p.drain("vector")
        p.op("vector", lambda v: v.tensor_scalar(
            cst[:, 36:38], cst[:, 36:38], 127.0, None, OP.mult))
        p.op("vector", lambda v: v.tensor_scalar(
            cst[:, 38:40], cst[:, 34:36], scl[:, 2:3], None, OP.mult))
        p.drain("vector")
        for sb in range(2):
            p.op("vector", lambda v, sb=sb: v.tensor_scalar(
                qf[:, D * sb:D * (sb + 1)], ysb[:, D * sb:D * (sb + 1)],
                cst[:, 36 + sb:37 + sb], R16, OP.mult, OP.add))
        p.drain("vector")
        p.op("vector", lambda v: v.tensor_scalar(
            qf[:, 0:2 * D], qf[:, 0:2 * D], -R16, None, OP.add))
        p.drain("vector")
        for sb in range(2):
            p.op("vector", lambda v, sb=sb: v.tensor_scalar(
                ysb[:, D * sb:D * (sb + 1)], qf[:, D * sb:D * (sb + 1)],
                cst[:, 38 + sb:39 + sb], None, OP.mult))
        return p.bar("vector", "sv")

    def front_chain(P, ybuf, xbuf, x2buf, xqbuf, qbuf):
        w = D
        ln_pow((ybuf, P))
        p.op("vector", lambda v: v.scalar_tensor_tensor(
            x2buf, ybuf, cst[0:P, 15:16], xbuf, OP.mult, OP.add))
        p.drain("vector")
        p.op("vector", lambda v: v.tensor_scalar(
            x2buf, x2buf, cst[0:P, 16:17], None, OP.add))
        p.drain("vector")
        ln_pow((x2buf, P))
        p.op("vector", lambda v: v.tensor_scalar(
            ybuf, x2buf, cst[0:P, 15:16], cst[0:P, 16:17], OP.mult, OP.add))
        p.drain("vector")
        p.op("vector", lambda v: v.tensor_reduce(
            cst[0:P, 17:18], ybuf, AX.X, OP.max, apply_absolute_value=True))
        p.drain("vector")
        p.op("vector", lambda v: v.tensor_scalar(
            cst[0:P, 17:18], cst[0:P, 17:18], 1e-5, None, OP.max))
        p.drain("vector")
        p.op("vector", lambda v: v.reciprocal(cst[0:P, 18:19],
                                              cst[0:P, 17:18]))
        p.drain("vector")
        p.op("vector", lambda v: v.tensor_scalar(
            cst[0:P, 18:19], cst[0:P, 18:19], 127.0, None, OP.mult))
        p.op("vector", lambda v: v.tensor_scalar(
            cst[0:P, 19:20], cst[0:P, 17:18], scl[0:P, 2:3], None, OP.mult))
        p.drain("vector")
        p.op("vector", lambda v: v.tensor_scalar(
            qbuf, ybuf, cst[0:P, 18:19], R16, OP.mult, OP.add))
        p.drain("vector")
        p.op("vector", lambda v: v.tensor_scalar(
            qbuf, qbuf, -R16, None, OP.add))
        p.drain("vector")
        p.op("vector", lambda v: v.tensor_scalar(
            xqbuf, qbuf, cst[0:P, 19:20], None, OP.mult))
        return p.bar("vector", "sv")

    if phase < 4:
        p.freeze()
    p.wait("tensor", "sv", scan_sv)
    p.wait("tensor", "sv", wq_sv)
    for hf in range(2):
        pbk = [p_ya, p_tr][hf]
        p.op("tensor", lambda t, hf=hf, pbk=pbk:
             t.matmul(pbk[0:2, 0:512], Ch[:, 0:2],
                      WoaB[:, 512 * hf:512 * (hf + 1)],
                      start=True, stop=True))
    mk = p.bar("tensor", "sp")
    p.wait("scalar", "sp", mk)
    for hf in range(2):
        pbk = [p_ya, p_tr][hf]
        p.op("scalar", lambda s, hf=hf, pbk=pbk: s.copy(
            ee[0:2, 512 * hf:512 * (hf + 1)], pbk[0:2, 0:512]))
    p.bar("scalar", "ss")
    p.wait("sync", "sv", p.mark("sv"))
    p.dma("sync", xe[0:1, :], dx[1:2, :], "dx")
    p.dma("sync", xe[1:2, :], dx[4098:4099, :], "dx")
    p.wait("vector", "ss", p.mark("ss"))
    p.wait("vector", "dx", p.mark("dx"))
    p.op("vector", lambda v: v.tensor_copy(ysb[0:2, 0:D], ee[0:2, 0:D]))
    p.bar("vector", "sv")
    esv = front_chain(2, ysb[0:2, 0:D], xe[0:2, :], x2t[0][0:2, 0:D],
                      ysb[0:2, 0:D], qf[0:2, 0:D])
    if debug:
        p.wait("sync", "sv", esv)
        p.dma("sync", dbg["d_edge"], x2t[0][0:2, 0:D].bitcast(F32), "ddb")
    p.wait("tensor", "sv", esv)
    for dc in range(8):
        p.op("tensor", lambda t, dc=dc: t.matmul(
            p_tr[:, 2 * dc:2 * (dc + 1)],
            ysb[0:2, 128 * dc:128 * (dc + 1)], identb[0:2, 0:2],
            start=True, stop=True))
    mk = p.bar("tensor", "sp")
    p.wait("vector", "sp", mk)
    p.op("vector", lambda v: v.tensor_copy(xqfm[:, 0:16], p_tr[:, 0:16]))
    esv = p.bar("vector", "sv")
    p.wait("tensor", "sv", esv)
    for hc in range(32):
        for dc in range(8):
            p.op("tensor", lambda t, hc=hc, dc=dc: t.matmul(
                p_m1[0][:, 2 * hc:2 * (hc + 1)],
                W1qT[:, HD * dc + 128 * hc:HD * dc + 128 * (hc + 1)],
                xqfm[:, 2 * dc:2 * (dc + 1)],
                start=(dc == 0), stop=(dc == 7)))
    mk = p.bar("tensor", "sp")
    p.wait("scalar", "sp", mk)
    p.op("scalar", lambda s: s.copy(Pedge[:, 0:64], p_m1[0][:, 0:64]))
    p.bar("scalar", "ss")
    p.wait("tensor", "ss", p.mark("ss"))
    p.wait("vector", "ss", p.mark("ss"))
    pe3 = Pedge[:, :].rearrange("q (c k) -> q c k", k=2)
    p.op("vector", lambda v: v.tensor_scalar(
        pe3[:, :, 0:1], pe3[:, :, 0:1], mleft[:, 0:1], None, OP.mult))
    p.op("vector", lambda v: v.tensor_scalar(
        pe3[:, :, 1:2], pe3[:, :, 1:2], mright[:, 0:1], None, OP.mult))
    edge_sv = p.bar("vector", "sv")

    m1ev = []
    m2k_done = {}
    fp_ev = {}
    tap_tv = {}
    yev = {}
    xmk = {}
    tail_sv = {}

    def fp_y(i):
        if i - 1 in tap_tv:
            p.wait("sync", "sv", tap_tv[i - 1])
        else:
            p.wait("sync", "sv", p.mark("sv"))
            p.wait("sync", "ss", p.mark("ss"))
        for s in range(2):
            row0 = 256 * i + 128 * s + 2
            p.dma("sync", xcf[:, D * s:D * (s + 1)], dx[row0:row0 + 128, :],
                  "dx")
        xmk[i] = p.mark("dx")
        if i - 1 in fp_ev:
            p.wait("scalar", "sv", fp_ev[i - 1])
            p.wait("tensor", "sv", fp_ev[i - 1])
        else:
            p.wait("scalar", "sv", p.mark("sv"))
            p.wait("tensor", "sv", p.mark("sv"))
        p.wait("tensor", "ss", p.mark("ss"))
        for s in range(2):
            if s == 1:
                p.wait("tensor", "ss", p.mark("ss"))
            for hf in range(2):
                pbk = [p_ya, p_tr][hf]
                p.op("tensor", lambda t, i=i, s=s, hf=hf, pbk=pbk: t.matmul(
                    pbk[:, 0:512], Ch[:, 256 * i + 128 * s + 2:
                                      256 * i + 128 * (s + 1) + 2],
                    WoaB[:, 512 * hf:512 * (hf + 1)], start=True, stop=True))
            mk = p.bar("tensor", "sp")
            p.wait("scalar", "sp", mk)
            for hf in range(2):
                pbk = [p_ya, p_tr][hf]
                p.op("scalar", lambda s_, s=s, hf=hf, pbk=pbk: s_.copy(
                    ysb[:, D * s + 512 * hf:D * s + 512 * (hf + 1)],
                    pbk[:, 0:512]))
            yev[i] = p.bar("scalar", "ss")
        if debug and i == 0:
            p.wait("sync", "ss", p.mark("ss"))
            p.dma("sync", dbg["d_y0"], ysb[:, 0:D].bitcast(F32), "ddb")

    def fp_chain(i):
        p.wait("vector", "ss", yev[i])
        p.wait("vector", "dx", xmk[i])
        fc = front_chain2(i)
        if debug and i == 0:
            p.wait("sync", "sv", fc)
            p.dma("sync", dbg["d_x2"], x2t[0][:, 0:D].bitcast(F32), "ddb")
        p.wait("tensor", "sv", fc)
        ev = None
        for s in range(2):
            for half in range(2):
                srcs = [ysb[:, D * s + 128 * (4 * half + q):
                            D * s + 128 * (4 * half + q + 1)]
                        for q in range(4)]
                dsts = [xqfm[:, 256 * (4 * half + q) + 128 * s:
                             256 * (4 * half + q) + 128 * (s + 1)]
                        for q in range(4)]
                ev = transpose_batch(srcs, dsts, identb[:, :], ev)
        return ev

    r3 = lambda t: t[:, :].rearrange("q (c w) -> q c w", w=PS)
    pe2 = Pedge[:, :].rearrange("q (c k) -> q c k", k=2)

    def m1_group(i, g, ev):
        Pi = Pb[i % 2]
        if g == 0:
            p.wait("tensor", "sv", ev)
            p.wait("scalar", "sv",
                   tail_sv[i - 2] if i - 2 in tail_sv else p.mark("sv"))
        for c16 in range(8):
            hc = 8 * g + c16
            pm = p_m1[hc % 2]
            if len(m1ev) >= 2:
                p.wait("tensor", "ss", m1ev[-2])
            for dc in range(8):
                p.op("tensor", lambda t, hc=hc, dc=dc, pm=pm:
                     t.matmul(pm[:, 0:256],
                              W1qT[:, HD * dc + 128 * hc:
                                   HD * dc + 128 * (hc + 1)],
                              xqfm[:, 256 * dc:256 * (dc + 1)],
                              start=(dc == 0), stop=(dc == 7)))
            mk = p.bar("tensor", "sp")
            p.wait("scalar", "sp", mk)
            p.op("scalar", lambda s, hc=hc, pm=pm, Pi=Pi: s.copy(
                Pi[:, PS * hc + 1:PS * hc + 257], pm[:, 0:256]))
            m1ev.append(p.bar("scalar", "ss"))
        p.wait("vector", "ss", p.mark("ss"))
        g8 = slice(8 * g, 8 * g + 8)
        if i == 0:
            p.op("vector", lambda v, g8=g8: v.tensor_copy(
                r3(Pb[0])[:, g8, 0:1], pe2[:, g8, 0:1]))
        else:
            p.op("vector", lambda v, i=i, g8=g8: v.tensor_copy(
                r3(Pb[i % 2])[:, g8, 0:1],
                r3(Pb[(i - 1) % 2])[:, g8, 256:257]))
            p.op("vector", lambda v, i=i, g8=g8: v.tensor_copy(
                r3(Pb[(i - 1) % 2])[:, g8, 257:258],
                r3(Pb[i % 2])[:, g8, 1:2]))
        return p.bar("vector", "sv")

    def tap_views(t, g):
        Pt = Pb[t % 2]
        rp_ = r3(Pt)
        base = 2048 * (g % 2)
        h3 = H2[:, base:base + 2048].rearrange("q (c w) -> q c w", w=256)
        xch = xcf[:, :].bitcast(F16)
        tmpA = xch[:, 0:2048].rearrange("q (c w) -> q c w", w=256)
        tmpB = xch[:, 2048:4096].rearrange("q (c w) -> q c w", w=256)
        w3 = cw2h[:, :].rearrange("q (c k) -> q c k", k=3)
        g8 = slice(8 * g, 8 * g + 8)
        return Pt, rp_, base, h3, tmpA, tmpB, w3, g8

    def patch_ops(t, g, level):
        Pt, rp_, base, h3, tmpA, tmpB, w3, g8 = tap_views(t, g)
        if level == 0:
            p.op("vector", lambda v, h3=h3, rp_=rp_, w3=w3, g8=g8:
                 v.tensor_tensor(h3[:, :, 255:256], rp_[:, g8, 256:257],
                                 w3[:, g8, 1:2], OP.mult))
            p.op("vector", lambda v, tmpA=tmpA, rp_=rp_, w3=w3, g8=g8:
                 v.tensor_tensor(tmpA[:, :, 255:256], rp_[:, g8, 255:256],
                                 w3[:, g8, 0:1], OP.mult))
        elif level == 1:
            p.op("vector", lambda v, h3=h3, tmpA=tmpA: v.tensor_tensor(
                h3[:, :, 255:256], h3[:, :, 255:256], tmpA[:, :, 255:256],
                OP.add))
            p.op("vector", lambda v, tmpB=tmpB, rp_=rp_, w3=w3, g8=g8:
                 v.tensor_tensor(tmpB[:, :, 255:256], rp_[:, g8, 257:258],
                                 w3[:, g8, 2:3], OP.mult))
        else:
            p.op("vector", lambda v, h3=h3, tmpB=tmpB: v.tensor_tensor(
                h3[:, :, 255:256], h3[:, :, 255:256], tmpB[:, :, 255:256],
                OP.add))

    def taps_main(t, g, sil_marks, halo_prev):
        Pt, rp_, base, h3, tmpA, tmpB, w3, g8 = tap_views(t, g)
        if g == 0:
            p.wait("vector", "dst", p.mark("dst"))
            if t - 1 in m2k_done:
                p.wait("vector", "sp", m2k_done[t - 1])
        a0, b0 = bass.broadcast_tensor_aps(rp_[:, g8, 0:255], w3[:, g8, 0:1])
        p.op("vector", lambda v, a=a0, b=b0, tmpA=tmpA: v.tensor_tensor(
            tmpA[:, :, 0:255], a, b, OP.mult))
        if g >= 2 and sil_marks:
            p.wait("vector", "ss", sil_marks[g - 2])
        a1, b1 = bass.broadcast_tensor_aps(rp_[:, g8, 1:256], w3[:, g8, 1:2])
        p.op("vector", lambda v, a=a1, b=b1, h3=h3: v.tensor_tensor(
            h3[:, :, 0:255], a, b, OP.mult))
        if halo_prev is not None:
            p.wait("vector", "sv", halo_prev)
            patch_ops(t, g - 1, 0)
        p.drain("vector")
        p.op("vector", lambda v, h3=h3, tmpA=tmpA: v.tensor_tensor(
            h3[:, :, 0:255], h3[:, :, 0:255], tmpA[:, :, 0:255], OP.add))
        a2, b2 = bass.broadcast_tensor_aps(rp_[:, g8, 2:257], w3[:, g8, 2:3])
        p.op("vector", lambda v, a=a2, b=b2, tmpB=tmpB: v.tensor_tensor(
            tmpB[:, :, 0:255], a, b, OP.mult))
        if halo_prev is not None:
            patch_ops(t, g - 1, 1)
        p.drain("vector")
        p.op("vector", lambda v, h3=h3, tmpB=tmpB: v.tensor_tensor(
            h3[:, :, 0:255], h3[:, :, 0:255], tmpB[:, :, 0:255], OP.add))
        if halo_prev is not None:
            patch_ops(t, g - 1, 2)
        return p.bar("vector", "sv")

    def silu_only(t, g, mark):
        Pt, rp_, base, h3, tmpA, tmpB, w3, g8 = tap_views(t, g)
        p.wait("scalar", "sv", mark)
        for c16 in range(8):
            c = 8 * g + c16
            p.op("scalar", lambda s_, c=c, c16=c16, base=base: s_.activation(
                Pt[:, PS * c + 2:PS * c + 258],
                H2[:, base + 256 * c16:base + 256 * (c16 + 1)],
                AF.Silu, bias=cb2[:, c:c + 1]))
        return p.bar("scalar", "ss")

    def taps_patch_silu(t, g, halo_sv):
        p.wait("vector", "sv", halo_sv)
        patch_ops(t, g, 0)
        p.drain("vector")
        patch_ops(t, g, 1)
        p.drain("vector")
        patch_ops(t, g, 2)
        tv = p.bar("vector", "sv")
        tap_tv[t] = tv
        return silu_only(t, g, tv)

    def back_head(t, sil):
        Pt = Pb[t % 2]
        rp = r3(Pt)
        if debug and t == 0:
            p.wait("sync", "ss", sil)
            p.dma("sync", dbg["d_P0"], Pt[:, :].bitcast(F32), "ddb")
        p.wait("vector", "ss", sil)
        p.wait("vector", "dst", p.mark("dst"))
        if t - 1 in m2k_done:
            p.wait("vector", "sp", m2k_done[t - 1])
        A3 = H2[:, 0:2048].rearrange("q (c w) -> q c w", w=256)
        B3 = H2[:, 2048:4096].rearrange("q (c w) -> q c w", w=256)
        p.op("vector", lambda v, A3=A3: v.tensor_tensor(
            A3, rp[:, 0:8, 2:258], rp[:, 8:16, 2:258], OP.max))
        p.op("vector", lambda v, B3=B3: v.tensor_tensor(
            B3, rp[:, 0:8, 2:258], rp[:, 8:16, 2:258], OP.min))
        p.drain("vector")
        for lo in (16, 24):
            p.op("vector", lambda v, lo=lo, A3=A3: v.tensor_tensor(
                A3, A3, rp[:, lo:lo + 8, 2:258], OP.max))
            p.op("vector", lambda v, lo=lo, B3=B3: v.tensor_tensor(
                B3, B3, rp[:, lo:lo + 8, 2:258], OP.min))
            p.drain("vector")
        for wd in (1024, 512, 256):
            p.op("vector", lambda v, wd=wd: v.tensor_tensor(
                H2[:, 0:wd], H2[:, 0:wd], H2[:, wd:2 * wd], OP.max))
            p.op("vector", lambda v, wd=wd: v.tensor_tensor(
                H2[:, 2048:2048 + wd], H2[:, 2048:2048 + wd],
                H2[:, 2048 + wd:2048 + 2 * wd], OP.min))
            p.drain("vector")
        p.op("vector", lambda v: v.scalar_tensor_tensor(
            H2[:, 0:256], H2[:, 2048:2304], -1.0, H2[:, 0:256],
            OP.mult, OP.max))
        mx = p.bar("vector", "sv")
        p.wait("tensor", "sv", mx)
        if t + 2 in yev:
            p.wait("tensor", "ss", yev[t + 2])
        for sb in range(2):
            p.op("tensor", lambda t_, sb=sb: t_.matmul(
                p_tr[:, 128 * sb:128 * (sb + 1)],
                H2[:, 128 * sb:128 * (sb + 1)],
                identh[:, :], start=True, stop=True))
        mk = p.bar("tensor", "sp")
        p.wait("vector", "sp", mk)
        for sb in range(2):
            p.op("vector", lambda v, sb=sb: v.tensor_reduce(
                cst[:, 20 + sb:21 + sb],
                p_tr[:, 128 * sb:128 * (sb + 1)], AX.X, OP.max))
        p.drain("vector")
        p.drain("vector")
        p.op("vector", lambda v: v.reciprocal(cst[:, 22:24], cst[:, 20:22]))
        p.drain("vector")
        p.op("vector", lambda v: v.tensor_scalar(
            s2h[:, 0:2], cst[:, 22:24], 127.0, 127.0 / ACLIP,
            OP.mult, OP.min))
        p.op("vector", lambda v: v.tensor_scalar(
            cst[:, 40:42], cst[:, 20:22], scl[:, 3:4], scl[:, 5:6],
            OP.mult, OP.max))
        sv = p.bar("vector", "sv")
        p.wait("tensor", "sv", sv)
        for sb in range(2):
            p.op("tensor", lambda t_, sb=sb: t_.matmul(
                p_tr[0:1, 128 * sb:128 * sb + 128],
                s2h[:, sb:sb + 1], identh[:, :],
                start=True, stop=True))
        mk = p.bar("tensor", "sp")
        p.wait("vector", "sp", mk)
        p.op("vector", lambda v: v.tensor_copy(srh[0:1, 0:256],
                                               p_tr[0:1, 0:256]))
        sv = p.bar("vector", "sv")
        p.wait("tensor", "sv", sv)
        for sb in range(2):
            p.op("tensor", lambda t_, sb=sb: t_.matmul(
                p_ya[:, 128 * sb:128 * (sb + 1)], oneh[:, :],
                srh[0:1, 128 * sb:128 * (sb + 1)], start=True, stop=True))
        mk = p.bar("tensor", "sp")
        p.wait("vector", "sp", mk)
        for sb in range(2):
            p.op("vector", lambda v, sb=sb: v.tensor_copy(
                s2b[:, 128 * sb:128 * (sb + 1)],
                p_ya[:, 128 * sb:128 * (sb + 1)]))
        sv = p.bar("vector", "sv")
        m2mk = {}
        for qq in range(4):
            for j in range(8):
                c = 8 * qq + j
                p.op("vector", lambda v, c=c: v.tensor_tensor(
                    Pt[:, PS * c + 2:PS * c + 258],
                    Pt[:, PS * c + 2:PS * c + 258], s2b[:, :], OP.mult))
            tq = p.bar("vector", "sv")
            p.wait("scalar", "sv", tq)
            sl3 = rp[:, 8 * qq:8 * qq + 8, 2:258]
            p.op("scalar", lambda s_, sl3=sl3: s_.activation(
                sl3, sl3, AF.Copy, bias=R16))
            p.op("scalar", lambda s_, sl3=sl3: s_.activation(
                sl3, sl3, AF.Copy, bias=-R16))
            hq = p.bar("scalar", "ss")
            p.wait("tensor", "ss", hq)
            for j in range(8):
                hc = 8 * qq + j
                for sb in range(2):
                    for hf in range(2):
                        p.op("tensor", lambda t_, hc=hc, sb=sb, hf=hf:
                             t_.matmul(
                                 p_m2[2 * sb + hf][:, 0:512],
                                 Pt[:, PS * hc + 2 + 128 * sb:
                                    PS * hc + 2 + 128 * (sb + 1)],
                                 W2qT[:, D * hc + 512 * hf:
                                      D * hc + 512 * (hf + 1)],
                                 start=(hc == 0), stop=(hc == 31)))
            m2mk[qq] = p.bar("tensor", "sp")
        m2k_done[t] = m2mk[3]
        p.wait("vector", "sp", m2mk[0])
        p.op("vector", lambda v, t=t: v.tensor_copy(
            x2keep(t)[:, :], x2t[t % 2][:, :]))
        p.bar("vector", "sv")

    def back_tail(t):
        p.wait("scalar", "sp", m2k_done[t])
        for sb in range(2):
            for hf in range(2):
                p.op("scalar", lambda s_, sb=sb, hf=hf: s_.activation(
                    ysbB[:, D * sb + 512 * hf:D * sb + 512 * (hf + 1)],
                    p_m2[2 * sb + hf][:, 0:512], AF.Copy,
                    scale=cst[:, 40 + sb:41 + sb]))
        ev = p.bar("scalar", "ss")
        p.wait("vector", "ss", ev)
        ln_pair(ysbB[:, 0:D], ysbB[:, D:2 * D])
        for sb in range(2):
            row0 = 256 * t + 128 * sb
            p.wait("vector", "dst", p.mark("dst"))
            for hf, ot in ((0, outt), (1, outt2)):
                p.op("vector", lambda v, sb=sb, hf=hf, ot=ot:
                     v.scalar_tensor_tensor(
                         ot[:, :],
                         ysbB[:, D * sb + 512 * hf:D * sb + 512 * (hf + 1)],
                         cst[:, 30 + sb:31 + sb],
                         x2keep(t)[:, D * sb + 512 * hf:
                                   D * sb + 512 * (hf + 1)],
                         OP.mult, OP.add))
            p.drain("vector")
            for hf, ot in ((0, outt), (1, outt2)):
                p.op("vector", lambda v, sb=sb, ot=ot: v.tensor_scalar(
                    ot[:, :], ot[:, :], cst[:, 32 + sb:33 + sb],
                    None, OP.subtract))
            sv = p.bar("vector", "sv")
            p.wait("sync", "sv", sv)
            p.dma("sync", dout[row0:row0 + 128, 0:512], outt[:, :], "dst")
            p.dma("sync", dout[row0:row0 + 128, 512:1024], outt2[:, :], "dst")
        tail_sv[t] = p.mark("sv")

    if phase < 5:
        p.freeze()
    p.wait("vector", "sv", edge_sv)
    sil_marks = []
    fp_y(0)
    for i in range(ntiles + 2):
        if i < ntiles:
            fp_ev[i] = fp_chain(i)
        if i >= 2:
            back_tail(i - 2)
        if i > ntiles:
            continue
        halos = {}
        for g in range(4):
            if i >= 1:
                tm = taps_main(i - 1, g, sil_marks, halos.get(g - 1))
            if i < ntiles:
                halo = m1_group(i, g, fp_ev[i])
            else:
                p.wait("vector", "ss", p.mark("ss"))
                g8 = slice(8 * g, 8 * g + 8)
                p.op("vector", lambda v, g8=g8: v.tensor_copy(
                    r3(Pb[(ntiles - 1) % 2])[:, g8, 257:258],
                    pe2[:, g8, 1:2]))
                halo = p.bar("vector", "sv")
            halos[g] = halo
            if i >= 1 and g >= 1:
                sil_marks.append(silu_only(i - 1, g - 1, tm))
        if i >= 1:
            sil_marks.append(taps_patch_silu(i - 1, 3, halos[3]))
        if i + 1 < ntiles:
            fp_y(i + 1)
        if i >= 1:
            back_head(i - 1, sil_marks[-1])
        sil_marks = []

    p.wait("sync", "dst", p.mark("dst"))
    p.wait("sync", "ddb", p.mark("ddb"))

    semctx = contextlib.ExitStack()
    sems = {s: semctx.enter_context(nc.semaphore(s)) for s in p.sem_names}
    with nc.Block() as block:
        @block.sync
        def _(e):
            p.replay("sync", e, sems)

        @block.vector
        def _(e):
            p.replay("vector", e, sems)

        @block.scalar
        def _(e):
            p.replay("scalar", e, sems)

        @block.tensor
        def _(e):
            p.replay("tensor", e, sems)

        @block.gpsimd
        def _(e):
            p.replay("gpsimd", e, sems)
    semctx.close()
    cctx.close()
    ctx.close()
    return nc


def _stage_inputs(inputs):
    x = np.asarray(inputs["L0_attn_x"], np.float32)
    w1 = np.ascontiguousarray(np.asarray(inputs["L0_ffn_w1"], np.float32))
    w2 = np.ascontiguousarray(np.asarray(inputs["L0_ffn_w2"], np.float32))
    win = np.ascontiguousarray(np.asarray(inputs["L0_attn_Win"], np.float32))
    binp = np.asarray(inputs["L0_attn_b_in"], np.float32).reshape(1, 80)
    cw1 = np.asarray(inputs["L0_attn_conv_w"], np.float32).reshape(D, 4)
    cw1 = np.ascontiguousarray(cw1.reshape(8, 128, 4).transpose(1, 0, 2)
                               .reshape(128, 32))
    cb1 = np.ascontiguousarray(
        np.asarray(inputs["L0_attn_conv_b"], np.float32).reshape(8, 128).T)
    woa = np.ascontiguousarray(np.concatenate(
        [np.asarray(inputs["L0_attn_Wout"], np.float32),
         np.asarray(inputs["L0_attn_b_out"], np.float32)[None, :]], axis=0))
    cw2 = np.asarray(inputs["L0_ffn_conv_w"], np.float32).reshape(HD, 3)
    cw2 = np.ascontiguousarray(cw2.reshape(32, 128, 3).transpose(1, 0, 2)
                               .reshape(128, 96))
    cb2 = np.ascontiguousarray(
        np.asarray(inputs["L0_ffn_conv_b"], np.float32).reshape(32, 128).T)
    alog = np.ascontiguousarray(
        np.asarray(inputs["L0_attn_A_log"], np.float32).reshape(16, 1))

    identf = np.eye(128, dtype=np.float32)
    onef = np.ones((1, 128), np.float32)
    onec = np.ones((128, 1), np.float32)

    in_maps = []
    for c in range(NCORES):
        s, j = c // 4, c % 4
        xs = np.zeros((XROWS, D), np.float32)
        lo = j * NT - 2
        a0, a1 = max(0, lo), min(16384, lo + XROWS)
        xs[a0 - lo:a1 - lo] = x[s, a0:a1]
        cm = np.zeros((16, 8), np.float32)
        for r in range(8):
            if r // 4 == s and r % 4 < j:
                cm[:, r] = 1.0
        in_maps.append(dict(
            x=xs, w1=w1, w2=w2, win=win, bin=binp, cw1=cw1, cb1=cb1,
            woa=woa, cw2=cw2, cb2=cb2, alog=alog,
            idf=identf, onef=onef, onec=onec,
            onesr=np.full((1, 2049), np.frombuffer(
                np.array([0x3F803F80], np.uint32).tobytes(),
                np.float32)[0], np.float32),
            constv=np.tile(np.array([[EPS, 1.0, RND32, -0.5, -1.0, 0.0,
                                      R16, -R16, -RND32]], np.float32),
                           (128, 1)),
            mconv=np.full((128, 1), 0.0 if j == 3 else 1.0, np.float32),
            mleft=np.full((128, 1), 0.0 if j == 0 else 1.0, np.float32),
            mright=np.full((128, 1), 0.0 if j == 3 else 1.0, np.float32),
            cmask=cm))
    return in_maps


_NC_CACHE = {}


def kernel(**inputs):
    in_maps = _stage_inputs(inputs)
    if "nc" not in _NC_CACHE:
        _NC_CACHE["nc"] = build_nc(debug=False)
    nc = _NC_CACHE["nc"]
    res = run_bass_kernel_spmd(nc, in_maps, core_ids=list(range(NCORES)))
    out = np.empty((2, 16384, D), np.float32)
    for c in range(NCORES):
        s, j = c // 4, c % 4
        out[s, j * NT:(j + 1) * NT] = res.results[c]["y_out"]
    return out
